# version 12
# speedup vs baseline: 1.4795x; 1.0515x over previous
import contextlib
import numpy as np
import ml_dtypes
import concourse.bass as bass
import concourse.mybir as mybir
from concourse.bass_utils import run_bass_kernel_spmd

F32 = mybir.dt.float32
BF16 = mybir.dt.bfloat16
U8 = mybir.dt.uint8
AF = mybir.ActivationFunctionType
ALU = mybir.AluOpType
AX = mybir.AxisListType

NCORES = 8
B_PER_CORE = 2
T = 2048
N_TOK = B_PER_CORE * T
D = 2048
NH = 16
DQ = 682
DQ_PAD = 768
DKV = 1024
DKR = 64
EPS = 1e-5
NE = 16
I_FF = 1408
GU = 2 * I_FF
N_QT = N_TOK // 512
N_TC = N_TOK // 128
SCALE = float(1.0 / np.sqrt(128.0))

ENGINES = ("PE", "DVE", "ACT", "POOL", "SP")
_ENG_ATTR = {"PE": "tensor", "DVE": "vector", "ACT": "scalar", "POOL": "gpsimd", "SP": "sync"}


class Builder:
    def __init__(self, nc, ctx):
        self.nc = nc
        self.ctx = ctx
        self.ops = {e: [] for e in ENGINES}
        self.phys = []
        self.free = {}
        self.cls = {}
        self.names = {}
        self.persistent = set()
        self.count = {}
        self.observed = {e: {} for e in ENGINES}
        self.regions = {}
        self.pending_reads = {e: [] for e in ENGINES}

    def sem(self, name, persistent=False, cls="hwdge"):
        pid = self.names.get(name)
        if pid is None:
            pool = self.free.setdefault(cls, [])
            if pool:
                pid = pool.pop()
            else:
                pid = len(self.phys)
                self.phys.append(self.ctx.enter_context(self.nc.semaphore(f"s{pid}")))
                self.count[pid] = 0
                self.cls[pid] = cls
            self.names[name] = pid
        if persistent:
            self.persistent.add(pid)
        return pid

    def _need_wait(self, engine, sem_name, val):
        if self.observed[engine].get(sem_name, 0) >= val:
            return None
        self.observed[engine][sem_name] = val
        return (sem_name, val)

    def _region(self, key):
        r = self.regions.get(key)
        if r is None:
            r = {"writer": None, "readers": []}
            self.regions[key] = r
        return r

    def op(self, engine, emit, reads=(), writes=(), war_only=(), inc=None, no_inc=False):
        waits = []
        for key in reads:
            r = self._region(key)
            if r["writer"] is not None:
                w = self._need_wait(engine, *r["writer"])
                if w:
                    waits.append(w)
        for key in tuple(writes) + tuple(war_only):
            r = self._region(key)
            if r["writer"] is not None:
                w = self._need_wait(engine, *r["writer"])
                if w:
                    waits.append(w)
            for rd in r["readers"]:
                w = self._need_wait(engine, *rd)
                if w:
                    waits.append(w)
        if inc is not None:
            sem_name, delta, qcls = inc
            pid = self.sem(sem_name, cls=qcls)
            self.count[pid] += delta
            marker = (pid, self.count[pid])
            incs = [(pid, delta)]
        elif not no_inc:
            pid = self.sem("e" + engine, persistent=True)
            self.count[pid] += 1
            marker = (pid, self.count[pid])
            incs = [(pid, 1)]
        else:
            marker = None
            incs = []
        if marker is not None:
            for key in writes:
                r = self._region(key)
                r["writer"] = marker
                r["readers"] = []
            for key in reads:
                self._region(key)["readers"].append(marker)
            for key in self.pending_reads[engine]:
                self._region(key)["readers"].append(marker)
            self.pending_reads[engine] = []
        else:
            assert not writes, "op with writes must carry a marker"
            self.pending_reads[engine].extend(reads)
        self.ops[engine].append((waits, emit, incs))

    def barrier(self, engines=ENGINES, release=False):
        targets = [(p, c) for p, c in self.count.items() if c > 0]
        for e in engines:
            waits = []
            for p, c in targets:
                w = self._need_wait(e, p, c)
                if w:
                    waits.append(w)
            if waits:
                self.ops[e].append((waits, None, []))
        if release:
            assert set(engines) == set(ENGINES)
            for name, pid in list(self.names.items()):
                if pid not in self.persistent:
                    del self.names[name]
                    self.free.setdefault(self.cls[pid], []).append(pid)

    def emit(self):
        nc = self.nc
        sems = self.phys
        with nc.Block() as block:
            for engine in ENGINES:
                ops = self.ops[engine]
                if not ops:
                    continue

                def body(eng, ops=ops):
                    for waits, emit, incs in ops:
                        for s, v in waits:
                            eng.wait_ge(sems[s], v)
                        if emit is not None:
                            inst = emit(eng)
                            for s, d in incs:
                                inst.then_inc(sems[s], d)

                getattr(block, _ENG_ATTR[engine])(body)


def build_host_constants():
    c = {}
    c["c_ident_bf"] = np.eye(128, dtype=ml_dtypes.bfloat16)
    c["c_ident_f32"] = np.eye(128, dtype=np.float32)
    masks = np.zeros((5, 128, 512), np.float32)
    for j in range(4):
        k = np.arange(128)[:, None] + j * 128
        q = np.arange(512)[None, :]
        masks[j] = np.where(k <= q, 0.0, -30000.0)
    c["c_masks"] = np.ascontiguousarray(masks.reshape(5 * 128, 512))
    c["c_iota16"] = np.ascontiguousarray(
        np.broadcast_to(np.arange(16, dtype=np.float32), (128, 16)))
    inv_freq = (1.0 / 10000.0) ** (np.arange(0, DKR, 2, dtype=np.float32) / DKR)
    freqs = np.arange(NH, dtype=np.float32)[:, None] * inv_freq[None, :]
    c["c_cosA"] = np.ascontiguousarray(
        np.broadcast_to(np.cos(freqs).reshape(1, 512), (128, 512))).astype(np.float32)
    c["c_sinA"] = np.ascontiguousarray(
        np.broadcast_to(np.sin(freqs).reshape(1, 512), (128, 512))).astype(np.float32)
    k = np.arange(128)
    c["c_tri"] = (k[:, None] < k[None, :]).astype(np.float32)
    c["c_iotacol"] = np.arange(128, dtype=np.float32).reshape(128, 1)
    return c


def row_ap(t, n, offset=0):
    return bass.AP(t, offset, [[n, 1], [1, n]])


def build_program(debug_outputs=(), phases=12):
    nc = bass.Bass("TRN2", target_bir_lowering=False)
    ctx = contextlib.ExitStack()
    b = Builder(nc, ctx)

    def dram(name, shape, dtype):
        kind = "ExternalOutput" if name in debug_outputs else "Internal"
        return nc.dram_tensor(name, shape, dtype, kind=kind)

    x_in = nc.dram_tensor("L0_attn_x", [N_TOK, D], F32, kind="ExternalInput")
    W_d_q = nc.dram_tensor("L0_attn_W_d_q", [DQ, D], F32, kind="ExternalInput")
    b_d_q = nc.dram_tensor("L0_attn_b_d_q", [DQ], F32, kind="ExternalInput")
    g_q = nc.dram_tensor("L0_attn_q_norm_g", [DQ], F32, kind="ExternalInput")
    bt_q = nc.dram_tensor("L0_attn_q_norm_b", [DQ], F32, kind="ExternalInput")
    W_u_q = nc.dram_tensor("L0_attn_W_u_q", [D, DQ], F32, kind="ExternalInput")
    b_u_q = nc.dram_tensor("L0_attn_b_u_q", [D], F32, kind="ExternalInput")
    W_d_kv = nc.dram_tensor("L0_attn_W_d_kv", [DKV + DKR, D], F32, kind="ExternalInput")
    b_d_kv = nc.dram_tensor("L0_attn_b_d_kv", [DKV + DKR], F32, kind="ExternalInput")
    g_kv = nc.dram_tensor("L0_attn_kv_norm_g", [DKV], F32, kind="ExternalInput")
    bt_kv = nc.dram_tensor("L0_attn_kv_norm_b", [DKV], F32, kind="ExternalInput")
    W_u_kv = nc.dram_tensor("L0_attn_W_u_kv", [3072, DKV], F32, kind="ExternalInput")
    b_u_kv = nc.dram_tensor("L0_attn_b_u_kv", [3072], F32, kind="ExternalInput")
    W_o = nc.dram_tensor("L0_attn_W_o", [D, D], F32, kind="ExternalInput")
    b_o = nc.dram_tensor("L0_attn_b_o", [D], F32, kind="ExternalInput")
    gate_w = nc.dram_tensor("L0_ffn_gate_w", [NE, D], F32, kind="ExternalInput")
    w_gate_up = nc.dram_tensor("L0_ffn_w_gate_up", [NE, D, GU], F32, kind="ExternalInput")
    w_down = nc.dram_tensor("L0_ffn_w_down", [NE, I_FF, D], F32, kind="ExternalInput")
    shared_gu = nc.dram_tensor("L0_ffn_shared_gate_up", [D, 2 * GU], F32, kind="ExternalInput")
    shared_dn = nc.dram_tensor("L0_ffn_shared_down", [GU, D], F32, kind="ExternalInput")
    ident_bf = nc.dram_tensor("c_ident_bf", [128, 128], BF16, kind="ExternalInput")
    ident_f32 = nc.dram_tensor("c_ident_f32", [128, 128], F32, kind="ExternalInput")
    masks_in = nc.dram_tensor("c_masks", [5 * 128, 512], F32, kind="ExternalInput")
    iota16_in = nc.dram_tensor("c_iota16", [128, 16], F32, kind="ExternalInput")
    cosA_in = nc.dram_tensor("c_cosA", [128, 512], F32, kind="ExternalInput")
    sinA_in = nc.dram_tensor("c_sinA", [128, 512], F32, kind="ExternalInput")
    tri_in = nc.dram_tensor("c_tri", [128, 128], F32, kind="ExternalInput")
    iotacol_in = nc.dram_tensor("c_iotacol", [128, 1], F32, kind="ExternalInput")
    out = nc.dram_tensor("out", [N_TOK, D], F32, kind="ExternalOutput")

    x_bf = dram("x_bf", [N_TOK, D], BF16)
    xT = dram("xT", [D, N_TOK], BF16)
    wdq_bf = dram("wdq_bf", [688, D], BF16)
    wdkv_bf = dram("wdkv_bf", [DKV + DKR, D], BF16)
    wukv_bf = dram("wukv_bf", [3072, DKV], BF16)
    wo_bf = dram("wo_bf", [D, D], BF16)
    gatew_bf = dram("gatew_bf", [NE, D], BF16)
    cq = dram("cq", [N_TOK, DQ_PAD], BF16)
    qT = dram("qT", [D, N_TOK], BF16)
    kvn = dram("kvn", [N_TOK, DKV], BF16)
    knT = dram("knT", [DKV, N_TOK], BF16)
    krT = dram("krT", [DKR, N_TOK], BF16)
    v_sc = dram("v_sc", [N_TOK, D], BF16)
    ynT = dram("ynT", [D, N_TOK], BF16)
    x1f = dram("x1f", [N_TOK, D], F32)
    x1bf = dram("x1bf", [N_TOK, D], BF16)
    x1T = dram("x1T", [D, N_TOK], BF16)
    yacc = dram("yacc", [N_TOK, D], F32)
    wgu_bf = dram("wgu_bf", [(NE + 2) * 16 * 128, GU], BF16)
    wdn_bf = dram("wdn_bf", [(NE + 2) * 11 * 128, D], BF16)
    CAP = 2048
    I32 = mybir.dt.int32
    DUMP = NE * CAP
    idx_tab = dram("idx_tab", [NE * CAP + 128], F32)
    tok2slot = dram("tok2slot", [N_TOK, 2], I32)
    xe_bf = dram("xe_bf", [2 * CAP, D], BF16)
    wtok = dram("wtok", [N_TOK, 2], F32)
    y_slab = dram("y_slab", [NE * CAP, D], F32)
    brot = dram("brot", [D], F32)

    PS = [nc.alloc_psum_tensor(f"ps{i}", [128, 512], F32) for i in range(6)]
    PSB = nc.alloc_psum_tensor("psb", [128, 1024], BF16)
    PSB2 = nc.alloc_psum_tensor("psb2", [128, 1024], BF16)

    sb = nc.alloc_sbuf_tensor
    ones_row_bf = sb("ones_row_bf", [1, 128], BF16)
    ones_row_f32 = sb("ones_row_f32", [1, 128], F32)
    ones_col_bf = sb("ones_col_bf", [128, 1], BF16)
    consts = sb("consts", [128, 2], F32)
    ident_bf_sb = sb("ident_bf_sb", [128, 128], BF16)
    ident_f32_sb = sb("ident_f32_sb", [128, 128], F32)

    rem = nc.sbuf_bytes_remaining
    arena_bytes = rem // 128 - 512 if rem > 128 * 250000 else rem - 512
    arena_base, _ = nc.bump_sbuf(arena_bytes)
    ar = {"off": 0, "n": 0, "peak": 0}

    def aalloc(name, shape, dtype):
        esz = {BF16: 2, mybir.dt.float16: 2, U8: 1, mybir.dt.int8: 1}.get(dtype, 4)
        per_part = (int(np.prod(shape[1:])) * esz + 31) & ~31
        off = ar["off"]
        assert off + per_part <= arena_bytes, \
            f"arena overflow at {name}: {off}+{per_part} > {arena_bytes}"
        ar["off"] = off + per_part
        ar["peak"] = max(ar["peak"], ar["off"])
        ar["n"] += 1
        return nc.alloc_sbuf_tensor_at(f"a{ar['n']}_{name}", shape, dtype,
                                       offset=arena_base + off)

    def arena_reset():
        b.barrier(release=True)
        ar["off"] = 0

    def dma(queue, dst_ap, src_ap, reads, writes, _sem_ignored=None,
            sem_name=None, **kw):
        if sem_name is None:
            sem_name = "Q_" + "_".join(str(p) for p in writes[0])
        qcls = "swdge" if queue == "POOL" else "hwdge"
        b.op(queue, lambda e: e.dma_start(dst_ap, src_ap, **kw),
             reads=reads, writes=writes, inc=(sem_name, 16, qcls))

    def dma_t(dst_ap, src_ap, reads, writes, _sem_ignored=None, queue="ACT"):
        sem_name = "Q_" + "_".join(str(p) for p in writes[0])
        b.op(queue, lambda e: e.dma_start(dst_ap, src_ap, transpose=True),
             reads=reads, writes=writes, inc=(sem_name, 16, "hwdge"))

    def ln_stats(stat, sqs, src_ap, dim, src_keys, tag, sqs_key):
        K = lambda i: (tag, "stat")
        b.op("DVE", lambda e: e.reduce_sum(stat[:, 0:1], src_ap, axis=AX.X),
             reads=src_keys, writes=[K(0)])
        b.op("ACT", lambda e: e.activation(sqs[:, 0:dim], src_ap, AF.Square,
                                           accum_out=stat[:, 1:2]),
             reads=src_keys + [K(0)], writes=[K(1), sqs_key])
        b.op("DVE", lambda e: e.tensor_scalar(stat[:, 2:4], stat[:, 0:2], 1.0 / dim, None,
                                              ALU.mult),
             reads=[K(1)], writes=[K(2)])
        b.op("DVE", lambda e: e.tensor_tensor(stat[:, 4:5], stat[:, 2:3], stat[:, 2:3],
                                              ALU.mult),
             reads=[K(2)], writes=[K(4)])
        b.op("DVE", lambda e: e.tensor_tensor(stat[:, 5:6], stat[:, 3:4], stat[:, 4:5],
                                              ALU.subtract),
             reads=[K(4)], writes=[K(5)])
        b.op("ACT", lambda e: e.activation(stat[:, 6:7], stat[:, 5:6], AF.Ln,
                                           bias=consts[:, 0:1]),
             reads=[K(5), ("consts",)], writes=[K(6)])
        b.op("ACT", lambda e: e.activation(stat[:, 7:8], stat[:, 6:7], AF.Exp,
                                           scale=consts[:, 1:2]),
             reads=[K(6), ("consts",)], writes=[K(7)])

    b.op("POOL", lambda e: e.memset(ones_row_bf[:, :], 1.0), writes=[("ones_row_bf",)])
    b.op("POOL", lambda e: e.memset(ones_row_f32[:, :], 1.0), writes=[("ones_row_f32",)])
    b.op("POOL", lambda e: e.memset(ones_col_bf[:, :], 1.0), writes=[("ones_col_bf",)])
    b.op("POOL", lambda e: e.memset(consts[:, 0:1], EPS), writes=[("consts",)])
    b.op("POOL", lambda e: e.memset(consts[:, 1:2], -0.5),
         reads=[("consts",)], writes=[("consts",)])
    dma("SP", ident_bf_sb[:, :], ident_bf.ap(), [], [("ident_bf_sb",)], "d_const")
    dma("SP", ident_f32_sb[:, :], ident_f32.ap(), [], [("ident_f32_sb",)], "d_const")

    for i in range(4):
        sl = slice(i * (N_TOK // 4), (i + 1) * (N_TOK // 4))
        dma("POOL", x_bf.ap()[sl, :], x_in.ap()[sl, :], [], [("x_bf", i)], "g_xbf")
    dma("POOL", wdq_bf.ap()[0:DQ, :], W_d_q.ap(), [], [("wdq_bf",)], "g_wbf")
    dma("POOL", wdkv_bf.ap(), W_d_kv.ap(), [], [("wdkv_bf",)], "g_wbf")
    dma("POOL", wukv_bf.ap()[0:DKV, :],
        bass.AP(W_u_kv, 0, [[192 * DKV, 16], [DKV, 64], [1, DKV]]),
        [], [("wukv_bf",)], "g_wbf")
    dma("POOL", wukv_bf.ap()[DKV:3072, :],
        bass.AP(W_u_kv, 64 * DKV, [[192 * DKV, 16], [DKV, 128], [1, DKV]]),
        [], [("wukv_bf",)], "g_wbf")
    dma("POOL", wo_bf.ap(), W_o.ap(), [], [("wo_bf",)], "g_wbf")
    dma("POOL", gatew_bf.ap(), gate_w.ap(), [], [("gatew_bf",)], "g_wbf")

    def emit_moe_staging(ei, pace=()):
        if True:
            pace = list(pace)
            if ei < NE:
                dma("POOL", wgu_bf.ap()[ei * 2048:(ei + 1) * 2048, :],
                    w_gate_up.ap()[ei, :, :], pace, [("wgu_bf", ei)], "g_mw")
                dma("POOL", wdn_bf.ap()[ei * 1408:(ei + 1) * 1408, :],
                    w_down.ap()[ei, :, :], pace, [("wdn_bf", ei)], "g_mw")
            else:
                half = ei - NE
                dma("POOL", bass.AP(wgu_bf, ei * 2048 * GU, [[GU, 2048], [1, I_FF]]),
                    shared_gu.ap()[:, half * I_FF:(half + 1) * I_FF],
                    pace, [("wgu_bf", ei)], "g_mw")
                dma("POOL", bass.AP(wgu_bf, ei * 2048 * GU + I_FF, [[GU, 2048], [1, I_FF]]),
                    shared_gu.ap()[:, GU + half * I_FF:GU + (half + 1) * I_FF],
                    pace, [("wgu_bf", ei)], "g_mw")
                dma("POOL", wdn_bf.ap()[ei * 1408:(ei + 1) * 1408, :],
                    shared_dn.ap()[half * I_FF:(half + 1) * I_FF, :],
                    pace, [("wdn_bf", ei)], "g_mw")
    zinit = aalloc("zinit", [128, D], F32)
    b.op("POOL", lambda e: e.memset(zinit[:, :], 0.0), writes=[("zinit",)])
    dma("SP", bass.AP(idx_tab, 0, [[(NE * CAP + 128) // 128, 128], [1, (NE * CAP + 128) // 128]]),
        zinit[:, 0:(NE * CAP + 128) // 128],
        [("zinit",)], [("idx_tab",)], "d_it")
    zpad = aalloc("zpad", [6, D], BF16)
    b.op("POOL", lambda e: e.memset(zpad[:, :], 0.0), writes=[("zpad",)])
    dma("POOL", wdq_bf.ap()[DQ:DQ + 6, :], zpad[:, :], [("zpad",)], [("wdq_bf",)], "g_wbf")

    xt_tile = [aalloc(f"xt_tile{i}", [128, 512], BF16) for i in range(2)]
    for c in range(16):
        for rb in range(8):
            slot = (c * 8 + rb) % 2
            tl = xt_tile[slot]
            dma_t(tl[:, :], x_bf.ap()[rb * 512:(rb + 1) * 512, c * 128:(c + 1) * 128],
                  [("x_bf", rb // 2)], [("xt_tile", slot)], "d_xti")
            dma("SP", xT.ap()[c * 128:(c + 1) * 128, rb * 512:(rb + 1) * 512], tl[:, :],
                [("xt_tile", slot)], [("xT", slot)], "d_xT")
    if phases <= 0:
        b.emit(); ctx.close(); return nc

    wdqT = [aalloc(f"wdqT{c}", [128, 688], BF16) for c in range(16)]
    for c in range(16):
        dma_t(wdqT[c][:, :], wdq_bf.ap()[:, c * 128:(c + 1) * 128],
              [("wdq_bf",)], [("wdqT", c)], "d_wdqT")
    bdq_row = aalloc("bdq_row", [1, DQ], BF16)
    dma("POOL", bdq_row[:, :], row_ap(b_d_q, DQ), [], [("bdq_row",)], "g_bias")
    gq_row = aalloc("gq_row", [1, DQ], F32)
    btq_row = aalloc("btq_row", [1, DQ], F32)
    dma("SP", gq_row[:, :], row_ap(g_q, DQ), [], [("gq_row",)], "d_bias")
    dma("SP", btq_row[:, :], row_ap(bt_q, DQ), [], [("btq_row",)], "d_bias")
    gq_bc = aalloc("gq_bc", [128, DQ], F32)
    btq_bc = aalloc("btq_bc", [128, DQ], F32)
    for row, bc, rkey, wkey in [(gq_row, gq_bc, "gq_row", "gq_bc"),
                                (btq_row, btq_bc, "btq_row", "btq_bc")]:
        for g0 in range(2):
            n = 512 if g0 == 0 else DQ - 512
            ps = PS[2]
            b.op("PE", lambda e, row=row, g0=g0, n=n, ps=ps: e.matmul(
                ps[:, 0:n], ones_row_f32[:, :], row[:, g0 * 512:g0 * 512 + n],
                start=True, stop=True),
                reads=[("ones_row_f32",), (rkey,)], war_only=[("ps_bc1",)],
                writes=[("ps_bc1",)])
            b.op("DVE", lambda e, bc=bc, g0=g0, n=n, ps=ps: e.tensor_copy(
                bc[:, g0 * 512:g0 * 512 + n], ps[:, 0:n]),
                reads=[("ps_bc1",)], writes=[(wkey,)])

    cq_f32 = [aalloc(f"cqf{i}", [128, DQ], F32) for i in range(2)]
    cq_bf = [aalloc(f"cqbf{i}", [128, DQ_PAD], BF16) for i in range(2)]
    sqs1 = aalloc("sqs1", [128, DQ], F32)
    stat1 = [aalloc(f"stat1_{i}", [128, 8], F32) for i in range(2)]
    xt_blk1 = [aalloc(f"xtb1_{i}", [128, 16 * 512], BF16) for i in range(2)]
    for i in range(2):
        b.op("POOL", lambda e, i=i: e.memset(cq_bf[i][:, DQ:DQ_PAD], 0.0),
             writes=[("cq_bf", i)])
    for tc in range(N_TC):
        slot = (tc // 4) % 2
        t4 = tc % 4
        xtb = xt_blk1[slot]
        if t4 == 0:
            src = bass.AP(xT, (tc // 4) * 512, [[N_TOK, 128], [128 * N_TOK, 16], [1, 512]])
            dma("SP", xtb[:, :], src, [("xT", 0), ("xT", 1)], [("xt_blk1", slot)], "d_p1l")
        for g0 in range(2):
            n = 512 if g0 == 0 else DQ - 512
            ps = PS[g0]
            for kc in range(16):
                b.op("PE", lambda e, ps=ps, kc=kc, g0=g0, n=n, xtb=xtb, t4=t4: e.matmul(
                    ps[:, 0:n], xtb[:, kc * 512 + t4 * 128:kc * 512 + (t4 + 1) * 128],
                    wdqT[kc][:, g0 * 512:g0 * 512 + n],
                    start=(kc == 0), stop=False),
                    reads=[("xt_blk1", (tc // 4) % 2), ("wdqT", kc)],
                    war_only=[("ps_cq", g0)] if kc == 0 else (),
                    no_inc=True)
            b.op("PE", lambda e, ps=ps, g0=g0, n=n: e.matmul(
                ps[:, 0:n], ones_row_bf[:, :], bdq_row[:, g0 * 512:g0 * 512 + n],
                start=False, stop=True),
                reads=[("ones_row_bf",), ("bdq_row",)], writes=[("ps_cq", g0)])
            b.op("DVE", lambda e, ps=ps, g0=g0, n=n, lslot=tc % 2: e.tensor_copy(
                cq_f32[lslot][:, g0 * 512:g0 * 512 + n], ps[:, 0:n]),
                reads=[("ps_cq", g0)], writes=[("cqf", tc % 2)])
        slot = tc % 2
        cf = cq_f32[slot]
        st1 = stat1[slot]
        ln_stats(st1, sqs1, cf[:, :], DQ, [("cqf", slot)], ("s1", slot), ("sqs1",))
        b.op("DVE", lambda e, cf=cf, st1=st1: e.tensor_scalar(cf[:, :], cf[:, :], st1[:, 2:3],
                                                              st1[:, 7:8], ALU.subtract, ALU.mult),
             reads=[("cqf", slot), (("s1", slot), "stat")], writes=[("cqf", slot)])
        b.op("DVE", lambda e, cf=cf: e.tensor_tensor(cf[:, :], cf[:, :], gq_bc[:, :], ALU.mult),
             reads=[("cqf", slot), ("gq_bc",)], writes=[("cqf", slot)])
        b.op("DVE", lambda e, cf=cf, slot=slot: e.tensor_tensor(
            cq_bf[slot][:, 0:DQ], cf[:, :], btq_bc[:, :], ALU.add),
            reads=[("cqf", slot), ("btq_bc",)], writes=[("cq_bf", slot)])
        dma("SP", cq.ap()[tc * 128:(tc + 1) * 128, :], cq_bf[slot][:, :],
            [("cq_bf", slot)], [("cq", slot)], "d_cq")
    arena_reset()
    if phases <= 1:
        b.emit(); ctx.close(); return nc

    wuq_f32 = [aalloc(f"wuqf{c}", [128, D], F32) for c in range(6)]
    wuqT_bf = [aalloc(f"wuqbf{c}", [128, D], BF16) for c in range(6)]
    wuq_ld = aalloc("wuq_ld", [128, DQ], F32)
    cosA_sb = aalloc("cosA_sb", [128, 512], F32)
    sinA_sb = aalloc("sinA_sb", [128, 512], F32)
    dma("SP", cosA_sb[:, :], cosA_in.ap(), [], [("cosA",)], "d_const")
    dma("SP", sinA_sb[:, :], sinA_in.ap(), [], [("sinA",)], "d_const")
    for rc in range(16):
        dma("SP", wuq_ld[:, :], W_u_q.ap()[rc * 128:(rc + 1) * 128, :],
            [("wuq_ld",)], [("wuq_ld",)], "d_wuq")
        for c in range(6):
            n = 128 if c < 5 else DQ - 640
            ps = PS[0]
            b.op("PE", lambda e, c=c, n=n, ps=ps: e.transpose(
                ps[0:n, 0:128], wuq_ld[:, c * 128:c * 128 + n], ident_f32_sb[:, :]),
                reads=[("wuq_ld",), ("ident_f32_sb",)], war_only=[("ps_tr",)],
                writes=[("ps_tr",)])
            b.op("DVE", lambda e, c=c, n=n, rc=rc, ps=ps: e.tensor_copy(
                wuq_f32[c][0:n, rc * 128:(rc + 1) * 128], ps[0:n, 0:128]),
                reads=[("ps_tr",)], writes=[("wuqf", c)])
    for c in range(6):
        w = wuq_f32[c]
        kk = 128 if c < 5 else DQ - 640
        A = bass.AP(w, 64, [[D, kk], [128, 16], [1, 32]])
        Bp = bass.AP(w, 96, [[D, kk], [128, 16], [1, 32]])
        cosAP = bass.AP(cosA_sb, 0, [[512, kk], [32, 16], [1, 32]])
        sinAP = bass.AP(sinA_sb, 0, [[512, kk], [32, 16], [1, 32]])
        if c == 0:
            _rt = [aalloc(f"rt{i}", [128, 512], F32) for i in range(3)]
        t1, t2, t3 = _rt
        t1ap = bass.AP(t1, 0, [[512, kk], [32, 16], [1, 32]])
        t2ap = bass.AP(t2, 0, [[512, kk], [32, 16], [1, 32]])
        t3ap = bass.AP(t3, 0, [[512, kk], [32, 16], [1, 32]])
        b.op("DVE", lambda e, A=A, cosAP=cosAP, t1ap=t1ap: e.tensor_tensor(
            t1ap, A, cosAP, ALU.mult),
            reads=[("wuqf", c), ("cosA",)], writes=[("rt1",)])
        b.op("DVE", lambda e, Bp=Bp, sinAP=sinAP, t2ap=t2ap: e.tensor_tensor(
            t2ap, Bp, sinAP, ALU.mult),
            reads=[("wuqf", c), ("sinA",)], writes=[("rt2",)])
        b.op("DVE", lambda e, A=A, sinAP=sinAP, t3ap=t3ap: e.tensor_tensor(
            t3ap, A, sinAP, ALU.mult),
            reads=[("wuqf", c), ("sinA",)], writes=[("rt3",)])
        b.op("DVE", lambda e, Bp=Bp, cosAP=cosAP: e.tensor_tensor(
            Bp, Bp, cosAP, ALU.mult),
            reads=[("wuqf", c), ("cosA",), ("rt2",), ("rt3",)], writes=[("wuqf", c)])
        b.op("DVE", lambda e, Bp=Bp, t3ap=t3ap: e.tensor_tensor(
            Bp, Bp, t3ap, ALU.subtract),
            reads=[("wuqf", c), ("rt3",)], writes=[("wuqf", c)])
        b.op("DVE", lambda e, A=A, t1ap=t1ap, t2ap=t2ap: e.tensor_tensor(
            A, t1ap, t2ap, ALU.add),
            reads=[("rt1",), ("rt2",), ("wuqf", c)], writes=[("wuqf", c)])
        b.op("ACT", lambda e, c=c, kk=kk: e.activation(
            wuqT_bf[c][0:kk, :], wuq_f32[c][0:kk, :], AF.Copy),
            reads=[("wuqf", c)], writes=[("wuqbf", c)])

    buq_row = aalloc("buq_row", [1, D], F32)
    dma("SP", buq_row[:, :], row_ap(b_u_q, D), [], [("buq_row",)], "d_bias")
    bA = bass.AP(buq_row, 64, [[D, 1], [128, 16], [1, 32]])
    bB = bass.AP(buq_row, 96, [[D, 1], [128, 16], [1, 32]])
    cos1 = bass.AP(cosA_sb, 0, [[512, 1], [32, 16], [1, 32]])
    sin1 = bass.AP(sinA_sb, 0, [[512, 1], [32, 16], [1, 32]])
    bt1 = aalloc("bt1", [1, 512], F32)
    bt2 = aalloc("bt2", [1, 512], F32)
    bt3 = aalloc("bt3", [1, 512], F32)
    bt1ap = bass.AP(bt1, 0, [[512, 1], [32, 16], [1, 32]])
    bt2ap = bass.AP(bt2, 0, [[512, 1], [32, 16], [1, 32]])
    bt3ap = bass.AP(bt3, 0, [[512, 1], [32, 16], [1, 32]])
    b.op("DVE", lambda e: e.tensor_tensor(bt1ap, bA, cos1, ALU.mult),
         reads=[("buq_row",), ("cosA",)], writes=[("bt1",)])
    b.op("DVE", lambda e: e.tensor_tensor(bt2ap, bB, sin1, ALU.mult),
         reads=[("buq_row",), ("sinA",)], writes=[("bt2",)])
    b.op("DVE", lambda e: e.tensor_tensor(bt3ap, bA, sin1, ALU.mult),
         reads=[("buq_row",), ("sinA",)], writes=[("bt3",)])
    b.op("DVE", lambda e: e.tensor_tensor(bB, bB, cos1, ALU.mult),
         reads=[("buq_row",), ("cosA",), ("bt2",), ("bt3",)], writes=[("buq_row",)])
    b.op("DVE", lambda e: e.tensor_tensor(bB, bB, bt3ap, ALU.subtract),
         reads=[("buq_row",), ("bt3",)], writes=[("buq_row",)])
    b.op("DVE", lambda e: e.tensor_tensor(bA, bt1ap, bt2ap, ALU.add),
         reads=[("bt1",), ("bt2",), ("buq_row",)], writes=[("buq_row",)])
    dma("SP", row_ap(brot, D), buq_row[:, :], [("buq_row",)], [("brot",)], "d_bias")
    buq_col = aalloc("buq_col", [128, 16], F32)
    for m in range(16):
        dma("SP", buq_col[:, m:m + 1], bass.AP(brot, m * 128, [[1, 128], [1, 1]]),
            [("brot",)], [("buq_col",)], "d_bias")

    cqT_t = [[aalloc(f"cqT{s}_{c}", [128, 512], BF16) for c in range(6)] for s in range(2)]
    q_ev = [aalloc(f"qev{i}", [128, 512], BF16) for i in range(2)]
    for qt in range(N_QT):
        slot = qt % 2
        for c in range(6):
            dma_t(cqT_t[slot][c][:, :],
                  cq.ap()[qt * 512:(qt + 1) * 512, c * 128:(c + 1) * 128],
                  [("cq", 0), ("cq", 1)], [("cqT", slot)], "d_cqT")
        for m in range(16):
            ps = PS[m % 2]
            for c in range(6):
                kk = 128 if c < 5 else DQ - 640
                b.op("PE", lambda e, ps=ps, c=c, m=m, kk=kk, slot=slot: e.matmul(
                    ps[:, :], wuqT_bf[c][0:kk, m * 128:(m + 1) * 128],
                    cqT_t[slot][c][0:kk, :],
                    start=(c == 0), stop=(c == 5)),
                    reads=[("wuqbf", c), ("cqT", slot)],
                    war_only=[("ps_q", m % 2)] if c == 0 else (),
                    writes=[("ps_q", m % 2)] if c == 5 else (),
                    no_inc=(c != 5))
            ev = q_ev[m % 2]
            b.op("DVE", lambda e, ps=ps, ev=ev, m=m: e.tensor_scalar(
                ev[:, :], ps[:, :], buq_col[:, m:m + 1], None, ALU.add),
                reads=[("ps_q", m % 2), ("buq_col",)], writes=[("q_ev", m % 2)])
            dma("SP", qT.ap()[m * 128:(m + 1) * 128, qt * 512:(qt + 1) * 512], ev[:, :],
                [("q_ev", m % 2)], [("qT", m % 2)], "d_qT")
    arena_reset()
    if phases <= 2:
        b.emit(); ctx.close(); return nc

    wdkvT = [aalloc(f"wdkvT{c}", [128, DKV + DKR], BF16) for c in range(16)]
    for c in range(16):
        dma_t(wdkvT[c][:, :], wdkv_bf.ap()[:, c * 128:(c + 1) * 128],
              [("wdkv_bf",)], [("wdkvT", c)], "d_wdkvT")
    bdkv_row = aalloc("bdkv_row", [1, DKV + DKR], BF16)
    dma("POOL", bdkv_row[:, :], row_ap(b_d_kv, DKV + DKR), [], [("bdkv_row",)], "g_bias")
    gkv_row = aalloc("gkv_row", [1, DKV], F32)
    btkv_row = aalloc("btkv_row", [1, DKV], F32)
    dma("SP", gkv_row[:, :], row_ap(g_kv, DKV), [], [("gkv_row",)], "d_bias")
    dma("SP", btkv_row[:, :], row_ap(bt_kv, DKV), [], [("btkv_row",)], "d_bias")
    gkv_bc = aalloc("gkv_bc", [128, DKV], F32)
    btkv_bc = aalloc("btkv_bc", [128, DKV], F32)
    for row, bc, rkey, wkey in [(gkv_row, gkv_bc, "gkv_row", "gkv_bc"),
                                (btkv_row, btkv_bc, "btkv_row", "btkv_bc")]:
        for g0 in range(2):
            ps = PS[5]
            b.op("PE", lambda e, row=row, g0=g0, ps=ps: e.matmul(
                ps[:, :], ones_row_f32[:, :], row[:, g0 * 512:(g0 + 1) * 512],
                start=True, stop=True),
                reads=[("ones_row_f32",), (rkey,)], war_only=[("ps_bc3",)],
                writes=[("ps_bc3",)])
            b.op("DVE", lambda e, bc=bc, g0=g0, ps=ps: e.tensor_copy(
                bc[:, g0 * 512:(g0 + 1) * 512], ps[:, :]),
                reads=[("ps_bc3",)], writes=[(wkey,)])

    ckv_f32 = [aalloc(f"ckvf{i}", [128, DKV + DKR], F32) for i in range(2)]
    kvn_bf = [aalloc(f"kvnbf{i}", [128, DKV], BF16) for i in range(2)]
    kr_bf = [aalloc(f"krbf{i}", [128, DKR], BF16) for i in range(2)]
    krT_sb = [aalloc(f"krTsb{i}", [64, 128], BF16) for i in range(2)]
    sqs3 = aalloc("sqs3", [128, DKV], F32)
    stat3 = [aalloc(f"stat3_{i}", [128, 8], F32) for i in range(2)]
    xt_blk3 = [aalloc(f"xtb3_{i}", [128, 16 * 512], BF16) for i in range(2)]
    groups3 = [(0, 512), (512, 512), (1024, 64)]
    for tc in range(N_TC):
        bslot = (tc // 4) % 2
        t4 = tc % 4
        xtb = xt_blk3[bslot]
        if t4 == 0:
            src = bass.AP(xT, (tc // 4) * 512, [[N_TOK, 128], [128 * N_TOK, 16], [1, 512]])
            dma("SP", xtb[:, :], src, [("xT", 0), ("xT", 1)], [("xt_blk3", bslot)], "d_p3l")
        slot = tc % 2
        for g0, (c0, n) in enumerate(groups3):
            ps = PS[2 + g0]
            for kc in range(16):
                b.op("PE", lambda e, ps=ps, kc=kc, c0=c0, n=n, xtb=xtb, t4=t4: e.matmul(
                    ps[:, 0:n], xtb[:, kc * 512 + t4 * 128:kc * 512 + (t4 + 1) * 128],
                    wdkvT[kc][:, c0:c0 + n],
                    start=(kc == 0), stop=False),
                    reads=[("xt_blk3", bslot), ("wdkvT", kc)],
                    war_only=[("ps_ckv", g0)] if kc == 0 else (),
                    no_inc=True)
            b.op("PE", lambda e, ps=ps, c0=c0, n=n: e.matmul(
                ps[:, 0:n], ones_row_bf[:, :], bdkv_row[:, c0:c0 + n],
                start=False, stop=True),
                reads=[("ones_row_bf",), ("bdkv_row",)], writes=[("ps_ckv", g0)])
            b.op("DVE", lambda e, ps=ps, c0=c0, n=n, slot=slot: e.tensor_copy(
                ckv_f32[slot][:, c0:c0 + n], ps[:, 0:n]),
                reads=[("ps_ckv", g0)], writes=[("ckvf", slot)])
        cf = ckv_f32[slot]
        st3 = stat3[slot]
        ln_stats(st3, sqs3, cf[:, 0:DKV], DKV, [("ckvf", slot)], ("s3", slot), ("sqs3",))
        b.op("ACT", lambda e, cf=cf, slot=slot: e.activation(
            kr_bf[slot][:, :], cf[:, DKV:DKV + DKR], AF.Copy),
            reads=[("ckvf", slot)], writes=[("kr_bf", slot)])
        b.op("DVE", lambda e, cf=cf, st3=st3: e.tensor_scalar(
            cf[:, 0:DKV], cf[:, 0:DKV], st3[:, 2:3], st3[:, 7:8],
            ALU.subtract, ALU.mult),
            reads=[("ckvf", slot), (("s3", slot), "stat")], writes=[("ckvf", slot)])
        b.op("DVE", lambda e, cf=cf: e.tensor_tensor(
            cf[:, 0:DKV], cf[:, 0:DKV], gkv_bc[:, :], ALU.mult),
            reads=[("ckvf", slot), ("gkv_bc",)], writes=[("ckvf", slot)])
        b.op("DVE", lambda e, cf=cf, slot=slot: e.tensor_tensor(
            kvn_bf[slot][:, :], cf[:, 0:DKV], btkv_bc[:, :], ALU.add),
            reads=[("ckvf", slot), ("btkv_bc",)], writes=[("kvn_bf", slot)])
        dma("SP", kvn.ap()[tc * 128:(tc + 1) * 128, :], kvn_bf[slot][:, :],
            [("kvn_bf", slot)], [("kvn", slot)], "d_kvn")
        b.op("PE", lambda e, slot=slot: e.transpose(
            PSB[0:64, 0:128], kr_bf[slot][:, :], ident_bf_sb[:, :]),
            reads=[("kr_bf", slot), ("ident_bf_sb",)], war_only=[("ps_krT",)],
            writes=[("ps_krT",)])
        b.op("DVE", lambda e, slot=slot: e.tensor_copy(krT_sb[slot][:, :], PSB[0:64, 0:128]),
             reads=[("ps_krT",)], writes=[("krT_sb", slot)])
        dma("SP", krT.ap()[:, tc * 128:(tc + 1) * 128], krT_sb[slot][:, :],
            [("krT_sb", slot)], [("krT", slot)], "d_krT")
    arena_reset()
    if phases <= 3:
        b.emit(); ctx.close(); return nc

    wukvT = [aalloc(f"wukvT{c}", [128, 3072], BF16) for c in range(8)]
    for c in range(8):
        for s4 in range(6):
            dma_t(wukvT[c][:, s4 * 512:(s4 + 1) * 512],
                  wukv_bf.ap()[s4 * 512:(s4 + 1) * 512, c * 128:(c + 1) * 128],
                  [("wukv_bf",)], [("wukvT", c)], "d_wukvT")
    bkn_col = aalloc("bkn_col", [128, 8], F32)
    for m in range(8):
        dma("SP", bkn_col[:, m:m + 1], bass.AP(b_u_kv, m * 2 * 192, [[192, 2], [1, 64]]),
            [("bkn_col",)], [("bkn_col",)], "d_bias")
    bv_row = aalloc("bv_row", [1, D], BF16)
    dma("POOL", bv_row[:, :], bass.AP(b_u_kv, 64, [[3072, 1], [192, 16], [1, 128]]),
        [], [("bv_row",)], "g_bias")

    kvnT_t = [[aalloc(f"kvnT{s}_{c}", [128, 512], BF16) for c in range(8)] for s in range(2)]
    kn_ev = [aalloc(f"knev{i}", [128, 512], BF16) for i in range(2)]
    for qt in range(N_QT):
        emit_moe_staging(qt, pace=[("kn_ev", qt % 2)] if qt >= 2 else [("kvnT", qt % 2)])
        slot = qt % 2
        for c in range(8):
            dma_t(kvnT_t[slot][c][:, :],
                  kvn.ap()[qt * 512:(qt + 1) * 512, c * 128:(c + 1) * 128],
                  [("kvn", 0), ("kvn", 1)], [("kvnT", slot)], "d_kvnT")
        for m in range(8):
            ps = PS[m % 2]
            for c in range(8):
                lh = wukvT[c][:, m * 128:(m + 1) * 128]
                b.op("PE", lambda e, ps=ps, c=c, lh=lh, slot=slot: e.matmul(
                    ps[:, :], lh, kvnT_t[slot][c][:, :],
                    start=(c == 0), stop=(c == 7)),
                    reads=[("wukvT", c), ("kvnT", slot)],
                    war_only=[("ps_kn", m % 2)] if c == 0 else (),
                    writes=[("ps_kn", m % 2)] if c == 7 else (),
                    no_inc=(c != 7))
            ev = kn_ev[m % 2]
            b.op("DVE", lambda e, ps=ps, ev=ev, m=m: e.tensor_scalar(
                ev[:, :], ps[:, :], bkn_col[:, m:m + 1], None, ALU.add),
                reads=[("ps_kn", m % 2), ("bkn_col",)], writes=[("kn_ev", m % 2)])
            dma("SP", knT.ap()[m * 128:(m + 1) * 128, qt * 512:(qt + 1) * 512], ev[:, :],
                [("kn_ev", m % 2)], [("knT", m % 2)], "d_knT")

    kvn_sta = [aalloc(f"kvnsta{i}", [128, 8 * 128], BF16) for i in range(2)]
    v_ev = [aalloc(f"vev{i}", [128, D], BF16) for i in range(2)]
    for tc in range(N_TC):
        if tc % 4 == 0:
            emit_moe_staging(8 + tc // 4, pace=[("v_ev", tc % 2)] if tc >= 2 else ())
        slot = tc % 2
        sta = kvn_sta[slot]
        for c in range(8):
            dma_t(sta[:, c * 128:(c + 1) * 128],
                  kvn.ap()[tc * 128:(tc + 1) * 128, c * 128:(c + 1) * 128],
                  [("kvn", 0), ("kvn", 1)], [("kvn_sta", slot)], "d_vl")
        for jg in range(4):
            ps = PS[2 + jg % 2]
            for c in range(8):
                rh = wukvT[c][:, DKV + jg * 512:DKV + (jg + 1) * 512]
                b.op("PE", lambda e, ps=ps, rh=rh, c=c, slot=slot, sta=sta: e.matmul(
                    ps[:, :], sta[:, c * 128:(c + 1) * 128], rh,
                    start=(c == 0), stop=False),
                    reads=[("kvn_sta", slot), ("wukvT", c)],
                    war_only=[("ps_v", jg % 2)] if c == 0 else (),
                    no_inc=True)
            b.op("PE", lambda e, ps=ps, jg=jg: e.matmul(
                ps[:, :], ones_row_bf[:, :], bv_row[:, jg * 512:(jg + 1) * 512],
                start=False, stop=True),
                reads=[("ones_row_bf",), ("bv_row",)], writes=[("ps_v", jg % 2)])
            b.op("DVE", lambda e, ps=ps, jg=jg, slot=slot: e.tensor_copy(
                v_ev[slot][:, jg * 512:(jg + 1) * 512], ps[:, :]),
                reads=[("ps_v", jg % 2)], writes=[("v_ev", slot)])
        dma("SP", v_sc.ap()[tc * 128:(tc + 1) * 128, :], v_ev[slot][:, :],
            [("v_ev", slot)], [("v_sc", slot)], "d_v")
    if phases <= 5:
        b.emit(); ctx.close(); return nc

    masks_sb = aalloc("masks_sb", [128, 5 * 512], F32)
    for j in range(5):
        dma("SP", masks_sb[:, j * 512:(j + 1) * 512],
            masks_in.ap()[j * 128:(j + 1) * 128, :],
            [("masks",)], [("masks",)], "d_const")
    q_blk = [aalloc(f"qblk{i}", [128, T], BF16) for i in range(2)]
    k_blk = [aalloc(f"kblk{i}", [128, T], BF16) for i in range(2)]
    v_seq = aalloc("v_seq", [128, 16 * 2048], BF16)
    sT_sb = [aalloc(f"sTsb{i}", [128, 512], F32) for i in range(4)]
    pT_sb = [aalloc(f"pTsb{i}", [128, 512], BF16) for i in range(6)]
    rcp_sb = [aalloc(f"rcp_sb{i}", [1, 512], F32) for i in range(2)]
    rcpb_sb = [aalloc(f"rcpb_sb{i}", [128, 512], F32) for i in range(2)]
    yn_ev = [aalloc(f"yn_ev{i}", [128, 512], BF16) for i in range(2)]
    vkeys = [("v_sc", 0), ("v_sc", 1)]

    for s in range(B_PER_CORE):
        for kc in range(16):
            dma("SP", v_seq[:, kc * 2048:(kc + 1) * 2048],
                v_sc.ap()[s * T + kc * 128:s * T + (kc + 1) * 128, :],
                vkeys + [("v_seq",)], [("v_seq",)], "d_av")
        for h in range(NH):
            if s == 0 and h < 2:
                emit_moe_staging(16 + h, pace=[("yn_ev", h % 2)])
            slot = (s * NH + h) % 2
            qb, kb = q_blk[slot], k_blk[slot]
            vb = None
            dma("SP", qb[:, :], qT.ap()[h * 128:(h + 1) * 128, s * T:(s + 1) * T],
                [("qT", 0), ("qT", 1)], [("q_blk", slot)], "d_aq")
            dma("SP", kb[0:64, :], knT.ap()[h * 64:(h + 1) * 64, s * T:(s + 1) * T],
                [("knT", 0), ("knT", 1)], [("k_blk", slot)], "d_ak")
            dma("SP", kb[64:128, :], krT.ap()[:, s * T:(s + 1) * T],
                [("krT", 0), ("krT", 1)], [("k_blk", slot)], "d_ak")
            tiles = []
            for qt in range(4):
                for kc in range(4 * qt + 4):
                    tiles.append((qt, kc))
            NT = len(tiles)
            LA = 4
            SB6 = [PS[0], PS[1], PS[5]]

            def emit_front(i):
                qt, kc = tiles[i]
                ps_s = SB6[i % 3]
                mj = kc - 4 * qt if kc >= 4 * qt else 4
                b.op("PE", lambda e, ps_s=ps_s, kc=kc, qt=qt, qb=qb, kb=kb: e.matmul(
                    ps_s[:, :], kb[:, kc * 128:(kc + 1) * 128],
                    qb[:, qt * 512:(qt + 1) * 512], start=True, stop=True),
                    reads=[("k_blk", slot), ("q_blk", slot)],
                    war_only=[("ps_s", i % 3)],
                    writes=[("ps_s", i % 3)])
                st = sT_sb[i % 4]
                b.op("DVE", lambda e, ps_s=ps_s, st=st, mj=mj: e.scalar_tensor_tensor(
                    st[:, :], ps_s[:, :], SCALE, masks_sb[:, mj * 512:(mj + 1) * 512],
                    ALU.mult, ALU.add),
                    reads=[("ps_s", i % 3), ("masks",)], writes=[("sT_sb", i % 4)])
                pt = pT_sb[i % 6]
                b.op("ACT", lambda e, st=st, pt=pt: e.activation(pt[:, :], st[:, :], AF.Exp),
                     reads=[("sT_sb", i % 4)], writes=[("pT_sb", i % 6)])

            def emit_back(j):
                qt, kc = tiles[j]
                nkc = 4 * qt + 4
                ps_y = PS[2 + qt % 2]
                ps_sum = PS[4]
                pt = pT_sb[j % 6]
                b.op("PE", lambda e, ps_y=ps_y, pt=pt, kc=kc, h=h, nkc=nkc: e.matmul(
                    ps_y[:, :], v_seq[:, kc * 2048 + h * 128:kc * 2048 + (h + 1) * 128],
                    pt[:, :],
                    start=(kc == 0), stop=(kc == nkc - 1)),
                    reads=[("pT_sb", j % 6), ("v_seq",)],
                    war_only=[("ps_y", qt % 2)] if kc == 0 else (),
                    writes=[("ps_y", qt % 2)] if kc == nkc - 1 else (),
                    no_inc=(kc != nkc - 1))
                b.op("PE", lambda e, ps_sum=ps_sum, pt=pt, kc=kc, nkc=nkc: e.matmul(
                    ps_sum[0:1, :], ones_col_bf[:, :], pt[:, :],
                    start=(kc == 0), stop=(kc == nkc - 1)),
                    reads=[("pT_sb", j % 6), ("ones_col_bf",)],
                    war_only=[("ps_sum",), ("ps_abc",)] if kc == 0 else (),
                    writes=[("ps_sum",)] if kc == nkc - 1 else (),
                    no_inc=(kc != nkc - 1))
                if kc != nkc - 1:
                    return
                qs = qt % 2
                b.op("DVE", lambda e, ps_sum=ps_sum, qs=qs: e.reciprocal(
                    rcp_sb[qs][:, :], ps_sum[0:1, :]),
                    reads=[("ps_sum",)], writes=[("rcp_sb", qs)])
                ps_bc = PS[4]
                b.op("PE", lambda e, ps_bc=ps_bc, qs=qs: e.matmul(
                    ps_bc[:, :], ones_row_f32[:, :], rcp_sb[qs][:, :], start=True, stop=True),
                    reads=[("ones_row_f32",), ("rcp_sb", qs)],
                    war_only=[("ps_abc",), ("ps_sum",)],
                    writes=[("ps_abc",)])
                b.op("DVE", lambda e, ps_bc=ps_bc, qs=qs: e.tensor_copy(
                    rcpb_sb[qs][:, :], ps_bc[:, :]),
                    reads=[("ps_abc",)], writes=[("rcpb_sb", qs)])
                b.op("DVE", lambda e, ps_y=ps_y, qs=qs: e.tensor_tensor(
                    yn_ev[qs][:, :], ps_y[:, :], rcpb_sb[qs][:, :], ALU.mult),
                    reads=[("ps_y", qt % 2), ("rcpb_sb", qs)], writes=[("yn_ev", qs)])
                dma("SP", ynT.ap()[h * 128:(h + 1) * 128,
                                   s * T + qt * 512:s * T + (qt + 1) * 512],
                    yn_ev[qs][:, :], [("yn_ev", qs)], [("ynT", qs)], "d_ynT")

            for i in range(NT + LA):
                if i < NT:
                    emit_front(i)
                j = i - LA
                if j >= 0:
                    emit_back(j)
    arena_reset()
    if phases <= 6:
        b.emit(); ctx.close(); return nc

    woT = [aalloc(f"woT{c}", [128, D], BF16) for c in range(16)]
    for c in range(16):
        for s4 in range(4):
            dma_t(woT[c][:, s4 * 512:(s4 + 1) * 512],
                  wo_bf.ap()[s4 * 512:(s4 + 1) * 512, c * 128:(c + 1) * 128],
                  [("wo_bf",)], [("woT", c)], "d_woT")
    bo_row = aalloc("bo_row", [1, D], BF16)
    dma("POOL", bo_row[:, :], row_ap(b_o, D), [], [("bo_row",)], "g_bias")
    yn_sta = [aalloc(f"ynsta{i}", [128, 16 * 512], BF16) for i in range(2)]
    x_chunk = [aalloc(f"xchunk{i}", [128, D], F32) for i in range(4)]
    r1 = [aalloc(f"r1_{i}", [128, D], F32) for i in range(4)]
    x1bf_t = [aalloc(f"x1bft{i}", [128, D], BF16) for i in range(4)]
    sqs7 = aalloc("sqs7", [128, D], F32)
    stat7 = [aalloc(f"stat7_{i}", [128, 8], F32) for i in range(4)]
    for tc in range(N_TC):
        bslot = (tc // 4) % 2
        t4 = tc % 4
        slot = tc % 4
        sta = yn_sta[bslot]
        if t4 == 0:
            src = bass.AP(ynT, (tc // 4) * 512, [[N_TOK, 128], [128 * N_TOK, 16], [1, 512]])
            dma("SP", sta[:, :], src, [("ynT", 0), ("ynT", 1)], [("yn_sta", bslot)], "d_p7l")
        dma("SP", x_chunk[slot][:, :], x_in.ap()[tc * 128:(tc + 1) * 128, :],
            [("x_chunk", slot)], [("x_chunk", slot)], "d_p7x")
        for jg in range(4):
            ps = PS[jg % 2]
            for c in range(16):
                b.op("PE", lambda e, ps=ps, c=c, jg=jg, sta=sta, t4=t4: e.matmul(
                    ps[:, :], sta[:, c * 512 + t4 * 128:c * 512 + (t4 + 1) * 128],
                    woT[c][:, jg * 512:(jg + 1) * 512],
                    start=(c == 0), stop=False),
                    reads=[("yn_sta", bslot), ("woT", c)],
                    war_only=[("ps_wo", jg % 2)] if c == 0 else (),
                    no_inc=True)
            b.op("PE", lambda e, ps=ps, jg=jg: e.matmul(
                ps[:, :], ones_row_bf[:, :], bo_row[:, jg * 512:(jg + 1) * 512],
                start=False, stop=True),
                reads=[("ones_row_bf",), ("bo_row",)], writes=[("ps_wo", jg % 2)])
            b.op("DVE", lambda e, ps=ps, jg=jg, slot=slot: e.tensor_tensor(
                r1[slot][:, jg * 512:(jg + 1) * 512], ps[:, :],
                x_chunk[slot][:, jg * 512:(jg + 1) * 512], ALU.add),
                reads=[("ps_wo", jg % 2), ("x_chunk", slot)], writes=[("r1", slot)])
        rf = r1[slot]
        st7 = stat7[slot]
        ln_stats(st7, sqs7, rf[:, :], D, [("r1", slot)], ("s7", slot), ("sqs7",))
        b.op("DVE", lambda e, rf=rf, st7=st7: e.tensor_scalar(rf[:, :], rf[:, :], st7[:, 2:3],
                                                              st7[:, 7:8], ALU.subtract, ALU.mult),
             reads=[("r1", slot), (("s7", slot), "stat")], writes=[("r1", slot)])
        b.op("ACT", lambda e, rf=rf, slot=slot: e.activation(
            x1bf_t[slot][:, :], rf[:, :], AF.Copy),
            reads=[("r1", slot)], writes=[("x1bf_t", slot)])
        dma("SP", x1f.ap()[tc * 128:(tc + 1) * 128, :], rf[:, :],
            [("r1", slot)], [("x1f", slot)], "d_x1f")
        dma("POOL", yacc.ap()[tc * 128:(tc + 1) * 128, :], rf[:, :],
            [("r1", slot)], [("yacc", tc)], "g_yacc0")
        dma("SP", x1bf.ap()[tc * 128:(tc + 1) * 128, :], x1bf_t[slot][:, :],
            [("x1bf_t", slot)], [("x1bf", slot)], "d_x1bf")

    x1t_tile = [aalloc(f"x1t{i}", [128, 512], BF16) for i in range(2)]
    for c in range(16):
        for rb in range(8):
            slot = (c * 8 + rb) % 2
            tl = x1t_tile[slot]
            dma_t(tl[:, :], x1bf.ap()[rb * 512:(rb + 1) * 512, c * 128:(c + 1) * 128],
                  [("x1bf", i) for i in range(4)], [("x1t_tile", slot)], "d_x1ti")
            dma("SP", x1T.ap()[c * 128:(c + 1) * 128, rb * 512:(rb + 1) * 512], tl[:, :],
                [("x1t_tile", slot)], [("x1T", slot)], "d_x1T")
    arena_reset()
    if phases <= 8:
        b.emit(); ctx.close(); return nc

    gwT = aalloc("gwT", [128, 16 * 16], BF16)
    for c in range(16):
        dma_t(gwT[:, c * 16:(c + 1) * 16], gatew_bf.ap()[:, c * 128:(c + 1) * 128],
              [("gatew_bf",)], [("gwT",)], "d_gwT")
    iota_sb = aalloc("iota_sb", [128, 16], F32)
    dma("SP", iota_sb[:, :], iota16_in.ap(), [], [("iota_sb",)], "d_const")
    x1t_blk = [aalloc(f"x1tb{i}", [128, 16 * 512], BF16) for i in range(2)]
    glog = aalloc("glog", [128, 16], F32)
    gsc = aalloc("gsc", [128, 16], F32)
    gstat = aalloc("gstat", [128, 6], F32)
    gmask = aalloc("gmask", [128, 16], U8)
    gmask2 = aalloc("gmask2", [128, 16], U8)
    gtmp = aalloc("gtmp", [128, 16], F32)
    gbig = aalloc("gbig", [128, 16], F32)
    b.op("POOL", lambda e: e.memset(gbig[:, :], 1e9), writes=[("gbig",)])
    tri_sb = aalloc("tri_sb", [128, 128], F32)
    dma("SP", tri_sb[:, :], tri_in.ap(), [], [("tri_sb",)], "d_tri")
    iotacol_sb = aalloc("iotacol_sb", [128, 1], F32)
    dma("SP", iotacol_sb[:, :], iotacol_in.ap(), [], [("iotacol_sb",)], "d_ic")
    p9ones = aalloc("p9ones", [128, 1], F32)
    b.op("POOL", lambda e: e.memset(p9ones[:, :], 1.0), writes=[("p9ones",)])
    prefix_row = aalloc("prefix_row", [1, 16], F32)
    b.op("POOL", lambda e: e.memset(prefix_row[:, :], 0.0), writes=[("prefix_row",)])
    gm1f = [aalloc(f"gm1f{i}", [128, 16], F32) for i in range(2)]
    gm2f = [aalloc(f"gm2f{i}", [128, 16], F32) for i in range(2)]
    gsel = [aalloc(f"gsel{i}", [128, 16], F32) for i in range(2)]
    gcum = [aalloc(f"gcum{i}", [128, 16], F32) for i in range(2)]
    gsc2s = [aalloc(f"gsc2s{i}", [128, 16], F32) for i in range(2)]
    gp = [aalloc(f"gp{i}", [128, 8], F32) for i in range(2)]
    gw = [aalloc(f"gw{i}", [128, 8], F32) for i in range(2)]
    goi = [aalloc(f"goi{i}", [128, 6], mybir.dt.int32) for i in range(2)]
    for tc in range(N_TC):
        bslot = (tc // 4) % 2
        t4 = tc % 4
        slot = tc % 2
        xtb = x1t_blk[bslot]
        if t4 == 0:
            src = bass.AP(x1T, (tc // 4) * 512, [[N_TOK, 128], [128 * N_TOK, 16], [1, 512]])
            dma("SP", xtb[:, :], src, [("x1T", 0), ("x1T", 1)], [("x1t_blk", bslot)], "d_p9l")
        ps = PS[0]
        for kc in range(16):
            b.op("PE", lambda e, ps=ps, kc=kc, xtb=xtb, t4=t4: e.matmul(
                ps[:, 0:16], xtb[:, kc * 512 + t4 * 128:kc * 512 + (t4 + 1) * 128],
                gwT[:, kc * 16:(kc + 1) * 16],
                start=(kc == 0), stop=(kc == 15)),
                reads=[("x1t_blk", bslot), ("gwT",)],
                war_only=[("ps_g",)] if kc == 0 else (),
                writes=[("ps_g",)] if kc == 15 else (),
                no_inc=(kc != 15))
        b.op("DVE", lambda e, ps=ps: e.tensor_copy(glog[:, :], ps[:, 0:16]),
             reads=[("ps_g",)], writes=[("glog",)])
        b.op("DVE", lambda e: e.reduce_max(gstat[:, 0:1], glog[:, :], axis=AX.X),
             reads=[("glog",)], writes=[("gstat",)])
        b.op("DVE", lambda e: e.tensor_scalar(glog[:, :], glog[:, :], gstat[:, 0:1], None,
                                              ALU.subtract),
             reads=[("glog",), ("gstat",)], writes=[("glog",)])
        b.op("ACT", lambda e: e.activation(gsc[:, :], glog[:, :], AF.Exp,
                                           accum_out=gstat[:, 1:2]),
             reads=[("glog",), ("gstat",)], writes=[("gsc",), ("gstat",)])
        b.op("DVE", lambda e: e.reciprocal(gstat[:, 2:3], gstat[:, 1:2]),
             reads=[("gstat",)], writes=[("gstat",)])
        b.op("DVE", lambda e: e.tensor_scalar(gsc[:, :], gsc[:, :], gstat[:, 2:3], None,
                                              ALU.mult),
             reads=[("gsc",), ("gstat",)], writes=[("gsc",)])
        b.op("DVE", lambda e: e.reduce_max(gstat[:, 3:4], gsc[:, :], axis=AX.X),
             reads=[("gsc",)], writes=[("gstat",)])
        b.op("DVE", lambda e: e.tensor_scalar(gmask[:, :], gsc[:, :], gstat[:, 3:4], None,
                                              ALU.is_equal),
             reads=[("gsc",), ("gstat",)], writes=[("gmask",)])
        b.op("DVE", lambda e: e.select(gtmp[:, :], gmask[:, :], iota_sb[:, :], gbig[:, :], add_drain=True),
             reads=[("gmask",), ("iota_sb",), ("gbig",)], writes=[("gtmp",)])
        b.op("DVE", lambda e: e.tensor_reduce(gstat[:, 4:5], gtmp[:, :], AX.X, ALU.min),
             reads=[("gtmp",)], writes=[("gstat",)])
        b.op("DVE", lambda e: e.tensor_scalar(gmask[:, :], iota_sb[:, :], gstat[:, 4:5], None,
                                              ALU.is_equal),
             reads=[("iota_sb",), ("gstat",)], writes=[("gmask",)])
        b.op("DVE", lambda e: e.tensor_copy(gtmp[:, :], gmask[:, :]),
             reads=[("gmask",)], writes=[("gtmp",)])
        b.op("DVE", lambda e: e.scalar_tensor_tensor(gtmp[:, :], gtmp[:, :], -2e9, gsc[:, :],
                                                     ALU.mult, ALU.add),
             reads=[("gtmp",), ("gsc",)], writes=[("gtmp",)])
        b.op("DVE", lambda e: e.reduce_max(gstat[:, 5:6], gtmp[:, :], axis=AX.X),
             reads=[("gtmp",)], writes=[("gstat",)])
        b.op("DVE", lambda e: e.tensor_scalar(gmask2[:, :], gtmp[:, :], gstat[:, 5:6], None,
                                              ALU.is_equal),
             reads=[("gtmp",), ("gstat",)], writes=[("gmask2",)])
        b.op("DVE", lambda e: e.select(gtmp[:, :], gmask2[:, :], iota_sb[:, :], gbig[:, :], add_drain=True),
             reads=[("gmask2",), ("iota_sb",), ("gbig",)], writes=[("gtmp",)])
        b.op("DVE", lambda e: e.tensor_reduce(gstat[:, 4:5], gtmp[:, :], AX.X, ALU.min),
             reads=[("gtmp",)], writes=[("gstat",)])
        b.op("DVE", lambda e: e.tensor_scalar(gmask2[:, :], iota_sb[:, :], gstat[:, 4:5], None,
                                              ALU.is_equal),
             reads=[("iota_sb",), ("gstat",)], writes=[("gmask2",)])
        m1, m2, sl, cm, sc2, p8, w8, oi = (gm1f[slot], gm2f[slot], gsel[slot],
                                           gcum[slot], gsc2s[slot], gp[slot],
                                           gw[slot], goi[slot])
        b.op("DVE", lambda e, m1=m1: e.tensor_copy(m1[:, :], gmask[:, :]),
             reads=[("gmask",)], writes=[("gm1f", slot)])
        b.op("DVE", lambda e, m2=m2: e.tensor_copy(m2[:, :], gmask2[:, :]),
             reads=[("gmask2",)], writes=[("gm2f", slot)])
        b.op("DVE", lambda e, sc2=sc2: e.tensor_copy(sc2[:, :], gsc[:, :]),
             reads=[("gsc",)], writes=[("gsc2s", slot)])
        b.op("DVE", lambda e, sl=sl, m1=m1, m2=m2: e.tensor_tensor(
            sl[:, :], m1[:, :], m2[:, :], ALU.add),
            reads=[("gm1f", slot), ("gm2f", slot)], writes=[("gsel", slot)])
        psC = PS[1]
        psT = PS[2]
        b.op("PE", lambda e, psC=psC, sl=sl: e.matmul(
            psC[:, 0:16], tri_sb[:, :], sl[:, :], start=True, stop=False),
            reads=[("tri_sb",), ("gsel", slot)], war_only=[("ps_cum",)], no_inc=True)
        b.op("PE", lambda e, psC=psC: e.matmul(
            psC[:, 0:16], ones_row_f32[:, :], prefix_row[:, :], start=False, stop=True),
            reads=[("ones_row_f32",), ("prefix_row",)], writes=[("ps_cum",)])
        b.op("PE", lambda e, psT=psT, sl=sl: e.matmul(
            psT[0:1, 0:16], p9ones[:, :], sl[:, :], start=True, stop=True),
            reads=[("p9ones",), ("gsel", slot)], war_only=[("ps_tot",)],
            writes=[("ps_tot",)])
        b.op("DVE", lambda e, cm=cm, psC=psC: e.tensor_copy(cm[:, :], psC[:, 0:16]),
             reads=[("ps_cum",)], writes=[("gcum", slot)])
        b.op("DVE", lambda e, psT=psT: e.tensor_tensor(
            prefix_row[:, :], prefix_row[:, :], psT[0:1, 0:16], ALU.add),
            reads=[("ps_tot",), ("prefix_row",)], writes=[("prefix_row",)])
        for ri, mk in ((0, m1), (1, m2)):
            b.op("DVE", lambda e, mk=mk, cm=cm: e.tensor_tensor(
                gtmp[:, :], cm[:, :], mk[:, :], ALU.mult),
                reads=[("gcum", slot), ("gm1f", slot), ("gm2f", slot)],
                writes=[("gtmp",)])
            b.op("DVE", lambda e, p8=p8, ri=ri: e.reduce_sum(
                p8[:, ri:ri + 1], gtmp[:, :], axis=AX.X),
                reads=[("gtmp",)], writes=[("gp", slot, "p", ri)])
            b.op("DVE", lambda e, mk=mk: e.tensor_tensor(
                gtmp[:, :], iota_sb[:, :], mk[:, :], ALU.mult),
                reads=[("iota_sb",), ("gm1f", slot), ("gm2f", slot), ("gp", slot, "p", ri)],
                writes=[("gtmp",)])
            b.op("DVE", lambda e, p8=p8, ri=ri: e.reduce_sum(
                p8[:, 2 + ri:3 + ri], gtmp[:, :], axis=AX.X),
                reads=[("gtmp",)], writes=[("gp", slot, "e", ri)])
            b.op("DVE", lambda e, mk=mk, sc2=sc2: e.tensor_tensor(
                gtmp[:, :], sc2[:, :], mk[:, :], ALU.mult),
                reads=[("gsc2s", slot), ("gm1f", slot), ("gm2f", slot),
                       ("gp", slot, "e", ri)],
                writes=[("gtmp",)])
            b.op("DVE", lambda e, w8=w8, ri=ri: e.reduce_sum(
                w8[:, ri:ri + 1], gtmp[:, :], axis=AX.X),
                reads=[("gtmp",)], writes=[("gw", slot, ri)])
            b.op("DVE", lambda e, p8=p8, ri=ri: e.tensor_scalar(
                p8[:, 4 + ri:5 + ri], p8[:, ri:ri + 1], float(CAP), None, ALU.is_ge),
                reads=[("gp", slot, "p", ri)], writes=[("gp", slot, "ov", ri)])
            b.op("DVE", lambda e, p8=p8, ri=ri: e.tensor_scalar(
                p8[:, 6 + ri:7 + ri], p8[:, 2 + ri:3 + ri], float(CAP), None, ALU.mult),
                reads=[("gp", slot, "e", ri)], writes=[("gp", slot, "o", ri)])
            b.op("DVE", lambda e, p8=p8, ri=ri: e.tensor_tensor(
                p8[:, 6 + ri:7 + ri], p8[:, 6 + ri:7 + ri], p8[:, ri:ri + 1], ALU.add),
                reads=[("gp", slot, "o", ri), ("gp", slot, "p", ri)],
                writes=[("gp", slot, "o", ri)])
            b.op("DVE", lambda e, w8=w8, p8=p8, ri=ri: e.tensor_tensor(
                w8[:, 4 + ri:5 + ri], p8[:, 6 + ri:7 + ri], p8[:, 4 + ri:5 + ri], ALU.mult),
                reads=[("gp", slot, "o", ri), ("gp", slot, "ov", ri)],
                writes=[("gw", slot, "t", ri)])
            b.op("DVE", lambda e, w8=w8, p8=p8, ri=ri: e.tensor_tensor(
                p8[:, 6 + ri:7 + ri], p8[:, 6 + ri:7 + ri], w8[:, 4 + ri:5 + ri], ALU.subtract),
                reads=[("gp", slot, "o", ri), ("gw", slot, "t", ri)],
                writes=[("gp", slot, "o", ri)])
            b.op("DVE", lambda e, p8=p8, ri=ri: e.scalar_tensor_tensor(
                p8[:, 6 + ri:7 + ri], p8[:, 4 + ri:5 + ri], float(DUMP), p8[:, 6 + ri:7 + ri],
                ALU.mult, ALU.add),
                reads=[("gp", slot, "o", ri), ("gp", slot, "ov", ri)],
                writes=[("gp", slot, "o", ri)])
            b.op("DVE", lambda e, oi=oi, p8=p8, ri=ri: e.tensor_copy(
                oi[:, ri:ri + 1], p8[:, 6 + ri:7 + ri]),
                reads=[("gp", slot, "o", ri)], writes=[("goi", slot, ri)])
            b.op("DVE", lambda e, p8=p8, w8=w8, ri=ri: e.tensor_scalar(
                w8[:, 6 + ri:7 + ri], p8[:, 4 + ri:5 + ri], -1.0, 1.0,
                ALU.mult, ALU.add),
                reads=[("gp", slot, "ov", ri)], writes=[("gw", slot, "keep", ri)])
            b.op("DVE", lambda e, p8=p8, w8=w8, ri=ri: e.tensor_tensor(
                w8[:, 4 + ri:5 + ri], p8[:, 6 + ri:7 + ri], w8[:, 6 + ri:7 + ri],
                ALU.mult),
                reads=[("gp", slot, "o", ri), ("gw", slot, "keep", ri)],
                writes=[("gw", slot, "t2s", ri)])
            b.op("DVE", lambda e, w8=w8, ri=ri: e.tensor_tensor(
                w8[:, ri:ri + 1], w8[:, ri:ri + 1], w8[:, 6 + ri:7 + ri], ALU.mult),
                reads=[("gw", slot, ri), ("gw", slot, "keep", ri)],
                writes=[("gw", slot, ri)])
            b.op("DVE", lambda e, oi=oi, w8=w8, ri=ri: e.tensor_copy(
                oi[:, 3 + ri:4 + ri], w8[:, 4 + ri:5 + ri]),
                reads=[("gw", slot, "t2s", ri)], writes=[("goi", slot, "t2s", ri)])
        b.op("DVE", lambda e, w8=w8, tc=tc: e.tensor_scalar(
            w8[:, 2:3], iotacol_sb[:, :], float(tc * 128), None, ALU.add),
            reads=[("iotacol_sb",)], writes=[("gw", slot, "tok")])
        b.op("DVE", lambda e, oi=oi, w8=w8: e.tensor_copy(oi[:, 2:3], w8[:, 2:3]),
             reads=[("gw", slot, "tok")], writes=[("goi", slot, "tok")])
        dma("SP", tok2slot.ap()[tc * 128:(tc + 1) * 128, :], oi[:, 3:5],
            [("goi", slot, "t2s", 0), ("goi", slot, "t2s", 1)], [("tok2slot",)], "d_t2s")
        dma("SP", wtok.ap()[tc * 128:(tc + 1) * 128, :], w8[:, 0:2],
            [("gw", slot, 0), ("gw", slot, 1)], [("wtok",)], "d_wtok")
        b.op("POOL", lambda e, oi=oi, w8=w8: e.indirect_dma_start(
            bass.AP(idx_tab, 0, [[1, NE * CAP + 128], [1, 1]]),
            bass.IndirectOffsetOnAxis(oi[:, 0:1], 0),
            w8[:, 2:3], None),
            reads=[("goi", slot, 0), ("gw", slot, "tok")],
            writes=[("idx_tab",)], inc=("g_itab", 16, "swdge"))
        b.op("POOL", lambda e, oi=oi, w8=w8: e.indirect_dma_start(
            bass.AP(idx_tab, 0, [[1, NE * CAP + 128], [1, 1]]),
            bass.IndirectOffsetOnAxis(oi[:, 1:2], 0),
            w8[:, 2:3], None),
            reads=[("goi", slot, 1), ("gw", slot, "tok")],
            writes=[("idx_tab",)], inc=("g_itab", 16, "swdge"))
    arena_reset()
    if phases <= 9:
        b.emit(); ctx.close(); return nc

    wgu_sb = aalloc("wgu_sb", [128, 16 * GU], BF16)
    wdn_sb = aalloc("wdn_sb", [128, 11 * D], BF16)
    x1t_moe = [aalloc(f"x1tm{i}", [128, 16 * 512], BF16) for i in range(2)]
    h_sb = [aalloc(f"hsb{i}", [128, 11 * 512], BF16) for i in range(2)]
    gsil = aalloc("gsil", [128, 512], BF16)
    y_evm_s = aalloc("yevm", [128, D], F32)
    y_evm = [y_evm_s, y_evm_s]
    ones_col_f32 = aalloc("ones_col_f32", [128, 1], F32)
    b.op("POOL", lambda e: e.memset(ones_col_f32[:, :], 1.0), writes=[("ones_col_f32",)])

    NBLK = NE * CAP // 128
    idxt_sb = aalloc("idxt_sb", [128, NBLK], mybir.dt.int32)
    idxt_f = aalloc("idxt_f", [128, 128], F32)
    for half in range(NBLK // 128):
        dma("SP", idxt_f[:, :],
            bass.AP(idx_tab, half * 128 * 128, [[128, 128], [1, 128]]),
            [("idx_tab",), ("idxt_f",)], [("idxt_f",)], "d_its")
        b.op("PE", lambda e: e.transpose(
            PS[2][:, 0:128], idxt_f[:, :], ident_f32_sb[:, :]),
            reads=[("idxt_f",), ("ident_f32_sb",)], war_only=[("ps_idxt",)],
            writes=[("ps_idxt",)])
        b.op("DVE", lambda e, half=half: e.tensor_copy(
            idxt_sb[:, half * 128:(half + 1) * 128], PS[2][:, 0:128]),
            reads=[("ps_idxt",)], writes=[("idxt_sb",)])
    xg_tok = [aalloc(f"xg_tok{i}", [128, D], BF16) for i in range(2)]
    for i in range(2):
        b.op("POOL", lambda e, i=i: e.memset(xg_tok[i][:, :], 0.0),
             writes=[("xg_tok", i)])

    N_EXP = NE + 2
    for ei in range(N_EXP):
        for kc in range(16):
            dma("SP", wgu_sb[:, kc * GU:(kc + 1) * GU],
                wgu_bf.ap()[ei * 2048 + kc * 128:ei * 2048 + (kc + 1) * 128, :],
                [("wgu_bf", ei)], [("wgu", kc)], "d_wgu")
        for ic in range(11):
            dma("SP", wdn_sb[:, ic * D:(ic + 1) * D],
                wdn_bf.ap()[ei * 1408 + ic * 128:ei * 1408 + (ic + 1) * 128, :],
                [("wdn_bf", ei)], [("wdn", ic)], "d_wdn")
        if ei < NE:
            nj = CAP // 512
            for c in range(CAP // 128):
                gs = c % 2
                b.op("POOL", lambda e, c=c, ei=ei, gs=gs: e.indirect_dma_start(
                    xg_tok[gs][:, :], None,
                    x1bf.ap(), bass.IndirectOffsetOnAxis(
                        idxt_sb[:, ei * (CAP // 128) + c:ei * (CAP // 128) + c + 1], 0)),
                    reads=[("idxt_sb",)] + [("x1bf", i) for i in range(4)],
                    war_only=[("xg_tok", gs)], writes=[("xg_tok", gs)],
                    inc=(f"g_xg{gs}", 16, "swdge"))
                dma("POOL", xe_bf.ap()[(ei % 2) * CAP + c * 128:(ei % 2) * CAP + (c + 1) * 128, :],
                    xg_tok[gs][:, :], [("xg_tok", gs)], [("xe_bf", ei % 2)],
                    sem_name=f"g_xep{ei % 2}")
        else:
            nj = N_QT
        for qt in range(nj + 1):
            if qt < nj:
                hslot = qt % 2
                xtb = x1t_moe[qt % 2]
                if ei < NE:
                    for kc in range(16):
                        dma_t(xtb[:, kc * 512:(kc + 1) * 512],
                              xe_bf.ap()[(ei % 2) * CAP + qt * 512:(ei % 2) * CAP + (qt + 1) * 512,
                                         kc * 128:(kc + 1) * 128],
                              [("xe_bf", ei % 2)], [("x1t_moe", qt % 2)], "d_xgt",
                              queue="SP")
                else:
                    src = bass.AP(x1T, qt * 512, [[N_TOK, 128], [128 * N_TOK, 16], [1, 512]])
                    dma("SP", xtb[:, :], src, [("x1T", 0), ("x1T", 1)],
                        [("x1t_moe", qt % 2)], "d_mx")
                for m in range(22):
                    ps = PS[m % 2]
                    for kc in range(16):
                        b.op("PE", lambda e, ps=ps, m=m, kc=kc, xtb=xtb: e.matmul(
                            ps[:, :], wgu_sb[:, kc * GU + m * 128:kc * GU + (m + 1) * 128],
                            xtb[:, kc * 512:(kc + 1) * 512],
                            start=(kc == 0), stop=(kc == 15)),
                            reads=[("wgu", kc), ("x1t_moe", qt % 2)],
                            war_only=[("ps_gu", m % 2)] if kc == 0 else (),
                            writes=[("ps_gu", m % 2)] if kc == 15 else (),
                            no_inc=(kc != 15))
                    if m < 11:
                        b.op("ACT", lambda e, ps=ps: e.activation(gsil[:, :], ps[:, :], AF.Silu),
                             reads=[("ps_gu", m % 2)], writes=[("gsil",)])
                        b.op("DVE", lambda e, m=m, hslot=hslot: e.tensor_copy(
                            h_sb[hslot][:, m * 512:(m + 1) * 512], gsil[:, :]),
                            reads=[("gsil",)], writes=[("h", hslot)])
                    else:
                        im = m - 11
                        b.op("DVE", lambda e, ps=ps, im=im, hslot=hslot: e.tensor_tensor(
                            h_sb[hslot][:, im * 512:(im + 1) * 512],
                            h_sb[hslot][:, im * 512:(im + 1) * 512], ps[:, :], ALU.mult),
                            reads=[("h", hslot), ("ps_gu", m % 2)], writes=[("h", hslot)])
            if qt >= 1:
                dqt = qt - 1
                dslot = dqt % 2
                for t4 in range(4):
                    tcg = dqt * 4 + t4
                    for ic in range(11):
                        for jg in range(4):
                            ps = PS[2 + jg]
                            b.op("PE", lambda e, ps=ps, ic=ic, t4=t4, jg=jg, dslot=dslot:
                                 e.matmul(
                                     ps[:, :],
                                     h_sb[dslot][:, ic * 512 + t4 * 128:
                                                 ic * 512 + (t4 + 1) * 128],
                                     wdn_sb[:, ic * D + jg * 512:ic * D + (jg + 1) * 512],
                                     start=(ic == 0), stop=(ic == 10)),
                                 reads=[("h", dslot), ("wdn", ic)],
                                 war_only=[("ps_dn", jg)] if ic == 0 else (),
                                 writes=[("ps_dn", jg)] if ic == 10 else (),
                                 no_inc=(ic != 10))
                    sc_ap = ones_col_f32[:, :]
                    sc_key = ("ones_col_f32",)
                    for jg in range(4):
                        ps = PS[2 + jg]
                        if jg % 2 == 0:
                            b.op("DVE", lambda e, ps=ps, jg=jg, t4=t4, sc_ap=sc_ap:
                                 e.tensor_scalar(y_evm[t4 % 2][:, jg * 512:(jg + 1) * 512],
                                                 ps[:, :], sc_ap, None, ALU.mult),
                                 reads=[("ps_dn", jg), sc_key],
                                 writes=[("y_evm",)])
                        else:
                            b.op("ACT", lambda e, ps=ps, jg=jg, t4=t4, sc_ap=sc_ap:
                                 e.activation(y_evm[t4 % 2][:, jg * 512:(jg + 1) * 512],
                                              ps[:, :], AF.Copy, scale=sc_ap),
                                 reads=[("ps_dn", jg), sc_key],
                                 writes=[("y_evm",)])
                    if ei < NE:
                        row0 = ei * CAP + dqt * 512 + t4 * 128
                        dma("ACT", y_slab.ap()[row0:row0 + 128, :],
                            y_evm[t4 % 2][:, :],
                            [("y_evm",)], [("y_slab", ei % 2)], "d_yslab")
                    else:
                        dma("POOL", yacc.ap()[tcg * 128:(tcg + 1) * 128, :],
                            y_evm[t4 % 2][:, :],
                            [("y_evm",)], [("yacc", tcg)], "g_yacc",
                            accum_op=ALU.add)
    arena_reset()
    if phases <= 10:
        b.emit(); ctx.close(); return nc

    yl = [aalloc(f"yl{i}", [128, D], F32) for i in range(2)]
    y1l = [aalloc(f"y1l{i}", [128, D], F32) for i in range(2)]
    y2l = [aalloc(f"y2l{i}", [128, D], F32) for i in range(2)]
    t2s_sb = [aalloc(f"t2s{i}", [128, 2], mybir.dt.int32) for i in range(2)]
    wv_sb = [aalloc(f"wv{i}", [128, 2], F32) for i in range(2)]
    sqs11 = aalloc("sqs11", [128, D], F32)
    stat11 = [aalloc(f"stat11_{i}", [128, 8], F32) for i in range(2)]
    slab_keys = [("y_slab", 0), ("y_slab", 1)]
    for tc in range(N_TC):
        slot = tc % 2
        dma("SP", yl[slot][:, :], yacc.ap()[tc * 128:(tc + 1) * 128, :],
            [("yl", slot), ("yacc", tc)], [("yl", slot)], "d_yl")
        dma("SP", t2s_sb[slot][:, :], tok2slot.ap()[tc * 128:(tc + 1) * 128, :],
            [("t2s", slot), ("tok2slot",)], [("t2s", slot)], "d_t2sl")
        dma("SP", wv_sb[slot][:, :], wtok.ap()[tc * 128:(tc + 1) * 128, :],
            [("wv", slot), ("wtok",)], [("wv", slot)], "d_wvl")
        b.op("POOL", lambda e, slot=slot: e.indirect_dma_start(
            y1l[slot][:, :], None,
            y_slab.ap(), bass.IndirectOffsetOnAxis(t2s_sb[slot][:, 0:1], 0)),
            reads=[("t2s", slot)] + slab_keys,
            war_only=[("y1l", slot)], writes=[("y1l", slot)],
            inc=(f"g_y1g{slot}", 16, "swdge"))
        b.op("POOL", lambda e, slot=slot: e.indirect_dma_start(
            y2l[slot][:, :], None,
            y_slab.ap(), bass.IndirectOffsetOnAxis(t2s_sb[slot][:, 1:2], 0)),
            reads=[("t2s", slot)] + slab_keys,
            war_only=[("y2l", slot)], writes=[("y2l", slot)],
            inc=(f"g_y2g{slot}", 16, "swdge"))
        rf = yl[slot]
        b.op("DVE", lambda e, rf=rf, slot=slot: e.scalar_tensor_tensor(
            rf[:, :], y1l[slot][:, :], wv_sb[slot][:, 0:1], rf[:, :],
            ALU.mult, ALU.add),
            reads=[("yl", slot), ("y1l", slot), ("wv", slot)], writes=[("yl", slot)])
        b.op("DVE", lambda e, rf=rf, slot=slot: e.scalar_tensor_tensor(
            rf[:, :], y2l[slot][:, :], wv_sb[slot][:, 1:2], rf[:, :],
            ALU.mult, ALU.add),
            reads=[("yl", slot), ("y2l", slot), ("wv", slot)], writes=[("yl", slot)])
        st11 = stat11[slot]
        ln_stats(st11, sqs11, rf[:, :], D, [("yl", slot)], ("s11", slot), ("sqs11",))
        b.op("DVE", lambda e, rf=rf, st11=st11: e.tensor_scalar(rf[:, :], rf[:, :], st11[:, 2:3],
                                                                st11[:, 7:8], ALU.subtract, ALU.mult),
             reads=[("yl", slot), (("s11", slot), "stat")], writes=[("yl", slot)])
        dma("SP", out.ap()[tc * 128:(tc + 1) * 128, :], rf[:, :],
            [("yl", slot)], [("out", tc)], "d_out")

    print("arena peak/cap:", ar["peak"], arena_bytes)
    b.barrier(engines=("SP",))
    b.emit()
    ctx.close()
    return nc


_nc_cache = {}


def kernel(**inputs):
    x = np.ascontiguousarray(np.asarray(inputs["L0_attn_x"], dtype=np.float32))
    B, T_, D_ = x.shape
    consts = build_host_constants()
    if "nc" not in _nc_cache:
        _nc_cache["nc"] = build_program()
    nc = _nc_cache["nc"]
    base = {k: np.ascontiguousarray(np.asarray(v, dtype=np.float32))
            for k, v in inputs.items() if k != "L0_attn_x"}
    in_maps = []
    for c in range(NCORES):
        m = {"L0_attn_x": np.ascontiguousarray(x[2 * c:2 * c + 2].reshape(N_TOK, D_))}
        m.update(base)
        m.update(consts)
        in_maps.append(m)
    res = run_bass_kernel_spmd(nc, in_maps, core_ids=list(range(NCORES)))
    outs = [np.asarray(res.results[c]["out"]).reshape(2, T_, D_) for c in range(NCORES)]
    return np.concatenate(outs, axis=0)


# version 14 (final)
# speedup vs baseline: 1.5395x; 1.0406x over previous
import contextlib
import numpy as np
import ml_dtypes
import concourse.bass as bass
import concourse.mybir as mybir
from concourse.bass_utils import run_bass_kernel_spmd

F32 = mybir.dt.float32
BF16 = mybir.dt.bfloat16
U8 = mybir.dt.uint8
AF = mybir.ActivationFunctionType
ALU = mybir.AluOpType
AX = mybir.AxisListType

NCORES = 8
B_PER_CORE = 2
T = 2048
N_TOK = B_PER_CORE * T
D = 2048
NH = 16
DQ = 682
DQ_PAD = 768
DKV = 1024
DKR = 64
EPS = 1e-5
NE = 16
I_FF = 1408
GU = 2 * I_FF
N_QT = N_TOK // 512
N_TC = N_TOK // 128
SCALE = float(1.0 / np.sqrt(128.0))

ENGINES = ("PE", "DVE", "ACT", "POOL", "SP")
_ENG_ATTR = {"PE": "tensor", "DVE": "vector", "ACT": "scalar", "POOL": "gpsimd", "SP": "sync"}


class Builder:
    def __init__(self, nc, ctx):
        self.nc = nc
        self.ctx = ctx
        self.ops = {e: [] for e in ENGINES}
        self.phys = []
        self.free = {}
        self.cls = {}
        self.names = {}
        self.persistent = set()
        self.count = {}
        self.observed = {e: {} for e in ENGINES}
        self.regions = {}
        self.pending_reads = {e: [] for e in ENGINES}

    def sem(self, name, persistent=False, cls="hwdge"):
        pid = self.names.get(name)
        if pid is None:
            pool = self.free.setdefault(cls, [])
            if pool:
                pid = pool.pop()
            else:
                pid = len(self.phys)
                self.phys.append(self.ctx.enter_context(self.nc.semaphore(f"s{pid}")))
                self.count[pid] = 0
                self.cls[pid] = cls
            self.names[name] = pid
        if persistent:
            self.persistent.add(pid)
        return pid

    def _need_wait(self, engine, sem_name, val):
        if self.observed[engine].get(sem_name, 0) >= val:
            return None
        self.observed[engine][sem_name] = val
        return (sem_name, val)

    def _region(self, key):
        r = self.regions.get(key)
        if r is None:
            r = {"writer": None, "readers": []}
            self.regions[key] = r
        return r

    def op(self, engine, emit, reads=(), writes=(), war_only=(), inc=None, no_inc=False):
        waits = []
        for key in reads:
            r = self._region(key)
            if r["writer"] is not None:
                w = self._need_wait(engine, *r["writer"])
                if w:
                    waits.append(w)
        for key in tuple(writes) + tuple(war_only):
            r = self._region(key)
            if r["writer"] is not None:
                w = self._need_wait(engine, *r["writer"])
                if w:
                    waits.append(w)
            for rd in r["readers"]:
                w = self._need_wait(engine, *rd)
                if w:
                    waits.append(w)
        if inc is not None:
            sem_name, delta, qcls = inc
            pid = self.sem(sem_name, cls=qcls)
            self.count[pid] += delta
            marker = (pid, self.count[pid])
            incs = [(pid, delta)]
        elif not no_inc:
            pid = self.sem("e" + engine, persistent=True)
            self.count[pid] += 1
            marker = (pid, self.count[pid])
            incs = [(pid, 1)]
        else:
            marker = None
            incs = []
        if marker is not None:
            for key in writes:
                r = self._region(key)
                r["writer"] = marker
                r["readers"] = []
            for key in reads:
                self._region(key)["readers"].append(marker)
            for key in self.pending_reads[engine]:
                self._region(key)["readers"].append(marker)
            self.pending_reads[engine] = []
        else:
            assert not writes, "op with writes must carry a marker"
            self.pending_reads[engine].extend(reads)
        self.ops[engine].append((waits, emit, incs))

    def barrier(self, engines=ENGINES, release=False):
        targets = [(p, c) for p, c in self.count.items() if c > 0]
        for e in engines:
            waits = []
            for p, c in targets:
                w = self._need_wait(e, p, c)
                if w:
                    waits.append(w)
            if waits:
                self.ops[e].append((waits, None, []))
        if release:
            assert set(engines) == set(ENGINES)
            for name, pid in list(self.names.items()):
                if pid not in self.persistent:
                    del self.names[name]
                    self.free.setdefault(self.cls[pid], []).append(pid)

    def emit(self):
        nc = self.nc
        sems = self.phys
        with nc.Block() as block:
            for engine in ENGINES:
                ops = self.ops[engine]
                if not ops:
                    continue

                def body(eng, ops=ops):
                    for waits, emit, incs in ops:
                        for s, v in waits:
                            eng.wait_ge(sems[s], v)
                        if emit is not None:
                            inst = emit(eng)
                            for s, d in incs:
                                inst.then_inc(sems[s], d)

                getattr(block, _ENG_ATTR[engine])(body)


def build_host_constants():
    c = {}
    c["c_ident_bf"] = np.eye(128, dtype=ml_dtypes.bfloat16)
    c["c_ident_f32"] = np.eye(128, dtype=np.float32)
    masks = np.zeros((5, 128, 512), np.float32)
    for j in range(4):
        k = np.arange(128)[:, None] + j * 128
        q = np.arange(512)[None, :]
        masks[j] = np.where(k <= q, 0.0, -30000.0)
    c["c_masks"] = np.ascontiguousarray(masks.reshape(5 * 128, 512))
    c["c_iota16"] = np.ascontiguousarray(
        np.broadcast_to(np.arange(16, dtype=np.float32), (128, 16)))
    inv_freq = (1.0 / 10000.0) ** (np.arange(0, DKR, 2, dtype=np.float32) / DKR)
    freqs = np.arange(NH, dtype=np.float32)[:, None] * inv_freq[None, :]
    c["c_cosA"] = np.ascontiguousarray(
        np.broadcast_to(np.cos(freqs).reshape(1, 512), (128, 512))).astype(np.float32)
    c["c_sinA"] = np.ascontiguousarray(
        np.broadcast_to(np.sin(freqs).reshape(1, 512), (128, 512))).astype(np.float32)
    k = np.arange(128)
    c["c_tri"] = (k[:, None] < k[None, :]).astype(np.float32)
    c["c_iotacol"] = np.arange(128, dtype=np.float32).reshape(128, 1)
    return c


def row_ap(t, n, offset=0):
    return bass.AP(t, offset, [[n, 1], [1, n]])


def build_program(debug_outputs=(), phases=12):
    nc = bass.Bass("TRN2", target_bir_lowering=False)
    ctx = contextlib.ExitStack()
    b = Builder(nc, ctx)

    def dram(name, shape, dtype):
        kind = "ExternalOutput" if name in debug_outputs else "Internal"
        return nc.dram_tensor(name, shape, dtype, kind=kind)

    x_in = nc.dram_tensor("L0_attn_x", [N_TOK, D], F32, kind="ExternalInput")
    W_d_q = nc.dram_tensor("L0_attn_W_d_q", [DQ, D], F32, kind="ExternalInput")
    b_d_q = nc.dram_tensor("L0_attn_b_d_q", [DQ], F32, kind="ExternalInput")
    g_q = nc.dram_tensor("L0_attn_q_norm_g", [DQ], F32, kind="ExternalInput")
    bt_q = nc.dram_tensor("L0_attn_q_norm_b", [DQ], F32, kind="ExternalInput")
    W_u_q = nc.dram_tensor("L0_attn_W_u_q", [D, DQ], F32, kind="ExternalInput")
    b_u_q = nc.dram_tensor("L0_attn_b_u_q", [D], F32, kind="ExternalInput")
    W_d_kv = nc.dram_tensor("L0_attn_W_d_kv", [DKV + DKR, D], F32, kind="ExternalInput")
    b_d_kv = nc.dram_tensor("L0_attn_b_d_kv", [DKV + DKR], F32, kind="ExternalInput")
    g_kv = nc.dram_tensor("L0_attn_kv_norm_g", [DKV], F32, kind="ExternalInput")
    bt_kv = nc.dram_tensor("L0_attn_kv_norm_b", [DKV], F32, kind="ExternalInput")
    W_u_kv = nc.dram_tensor("L0_attn_W_u_kv", [3072, DKV], F32, kind="ExternalInput")
    b_u_kv = nc.dram_tensor("L0_attn_b_u_kv", [3072], F32, kind="ExternalInput")
    W_o = nc.dram_tensor("L0_attn_W_o", [D, D], F32, kind="ExternalInput")
    b_o = nc.dram_tensor("L0_attn_b_o", [D], F32, kind="ExternalInput")
    gate_w = nc.dram_tensor("L0_ffn_gate_w", [NE, D], F32, kind="ExternalInput")
    w_gate_up = nc.dram_tensor("L0_ffn_w_gate_up", [NE, D, GU], F32, kind="ExternalInput")
    w_down = nc.dram_tensor("L0_ffn_w_down", [NE, I_FF, D], F32, kind="ExternalInput")
    shared_gu = nc.dram_tensor("L0_ffn_shared_gate_up", [D, 2 * GU], F32, kind="ExternalInput")
    shared_dn = nc.dram_tensor("L0_ffn_shared_down", [GU, D], F32, kind="ExternalInput")
    ident_bf = nc.dram_tensor("c_ident_bf", [128, 128], BF16, kind="ExternalInput")
    ident_f32 = nc.dram_tensor("c_ident_f32", [128, 128], F32, kind="ExternalInput")
    masks_in = nc.dram_tensor("c_masks", [5 * 128, 512], F32, kind="ExternalInput")
    iota16_in = nc.dram_tensor("c_iota16", [128, 16], F32, kind="ExternalInput")
    cosA_in = nc.dram_tensor("c_cosA", [128, 512], F32, kind="ExternalInput")
    sinA_in = nc.dram_tensor("c_sinA", [128, 512], F32, kind="ExternalInput")
    tri_in = nc.dram_tensor("c_tri", [128, 128], F32, kind="ExternalInput")
    iotacol_in = nc.dram_tensor("c_iotacol", [128, 1], F32, kind="ExternalInput")
    out = nc.dram_tensor("out", [N_TOK, D], F32, kind="ExternalOutput")

    x_bf = dram("x_bf", [N_TOK, D], BF16)
    xT = dram("xT", [D, N_TOK], BF16)
    wdq_bf = dram("wdq_bf", [688, D], BF16)
    wdkv_bf = dram("wdkv_bf", [DKV + DKR, D], BF16)
    wukv_bf = dram("wukv_bf", [3072, DKV], BF16)
    wo_bf = dram("wo_bf", [D, D], BF16)
    gatew_bf = dram("gatew_bf", [NE, D], BF16)
    cq = dram("cq", [N_TOK, DQ_PAD], BF16)
    qT = dram("qT", [D, N_TOK], BF16)
    kvn = dram("kvn", [N_TOK, DKV], BF16)
    knT = dram("knT", [DKV, N_TOK], BF16)
    krT = dram("krT", [DKR, N_TOK], BF16)
    v_sc = dram("v_sc", [N_TOK, D], BF16)
    ynT = dram("ynT", [D, N_TOK], BF16)
    x1f = dram("x1f", [N_TOK, D], F32)
    x1bf = dram("x1bf", [N_TOK, D], BF16)
    x1T = dram("x1T", [D, N_TOK], BF16)
    yacc = dram("yacc", [N_TOK, D], F32)
    wgu_bf = dram("wgu_bf", [(NE + 2) * 16 * 128, GU], BF16)
    wdn_bf = dram("wdn_bf", [(NE + 2) * 11 * 128, D], BF16)
    CAP = 2048
    I32 = mybir.dt.int32
    DUMP = NE * CAP
    idx_tab = dram("idx_tab", [NE * CAP + 128], F32)
    tok2slot = dram("tok2slot", [N_TOK, 2], I32)
    xe_bf = dram("xe_bf", [2 * CAP, D], BF16)
    wtok = dram("wtok", [N_TOK, 2], F32)
    y_slab = dram("y_slab", [NE * CAP, D], F32)
    brot = dram("brot", [D], F32)

    PS = [nc.alloc_psum_tensor(f"ps{i}", [128, 512], F32) for i in range(6)]
    PSB = nc.alloc_psum_tensor("psb", [128, 1024], BF16)
    PSB2 = nc.alloc_psum_tensor("psb2", [128, 1024], BF16)

    sb = nc.alloc_sbuf_tensor
    ones_row_bf = sb("ones_row_bf", [1, 128], BF16)
    ones_row_f32 = sb("ones_row_f32", [1, 128], F32)
    ones_col_bf = sb("ones_col_bf", [128, 1], BF16)
    consts = sb("consts", [128, 2], F32)
    ident_bf_sb = sb("ident_bf_sb", [128, 128], BF16)
    ident_f32_sb = sb("ident_f32_sb", [128, 128], F32)

    rem = nc.sbuf_bytes_remaining
    arena_bytes = rem // 128 - 512 if rem > 128 * 250000 else rem - 512
    arena_base, _ = nc.bump_sbuf(arena_bytes)
    ar = {"off": 0, "n": 0, "peak": 0}

    def aalloc(name, shape, dtype):
        esz = {BF16: 2, mybir.dt.float16: 2, U8: 1, mybir.dt.int8: 1}.get(dtype, 4)
        per_part = (int(np.prod(shape[1:])) * esz + 31) & ~31
        off = ar["off"]
        assert off + per_part <= arena_bytes, \
            f"arena overflow at {name}: {off}+{per_part} > {arena_bytes}"
        ar["off"] = off + per_part
        ar["peak"] = max(ar["peak"], ar["off"])
        ar["n"] += 1
        return nc.alloc_sbuf_tensor_at(f"a{ar['n']}_{name}", shape, dtype,
                                       offset=arena_base + off)

    def arena_reset():
        b.barrier(release=True)
        ar["off"] = 0

    def dma(queue, dst_ap, src_ap, reads, writes, _sem_ignored=None,
            sem_name=None, **kw):
        if sem_name is None:
            sem_name = "Q_" + "_".join(str(p) for p in writes[0])
        qcls = "swdge" if queue == "POOL" else "hwdge"
        b.op(queue, lambda e: e.dma_start(dst_ap, src_ap, **kw),
             reads=reads, writes=writes, inc=(sem_name, 16, qcls))

    def dma_t(dst_ap, src_ap, reads, writes, _sem_ignored=None, queue="ACT"):
        sem_name = "Q_" + "_".join(str(p) for p in writes[0])
        b.op(queue, lambda e: e.dma_start(dst_ap, src_ap, transpose=True),
             reads=reads, writes=writes, inc=(sem_name, 16, "hwdge"))

    def ln_stats(stat, sqs, src_ap, dim, src_keys, tag, sqs_key):
        K = lambda i: (tag, "stat")
        b.op("DVE", lambda e: e.reduce_sum(stat[:, 0:1], src_ap, axis=AX.X),
             reads=src_keys, writes=[K(0)])
        b.op("ACT", lambda e: e.activation(sqs[:, 0:dim], src_ap, AF.Square,
                                           accum_out=stat[:, 1:2]),
             reads=src_keys + [K(0)], writes=[K(1), sqs_key])
        b.op("DVE", lambda e: e.tensor_scalar(stat[:, 2:4], stat[:, 0:2], 1.0 / dim, None,
                                              ALU.mult),
             reads=[K(1)], writes=[K(2)])
        b.op("DVE", lambda e: e.tensor_tensor(stat[:, 4:5], stat[:, 2:3], stat[:, 2:3],
                                              ALU.mult),
             reads=[K(2)], writes=[K(4)])
        b.op("DVE", lambda e: e.tensor_tensor(stat[:, 5:6], stat[:, 3:4], stat[:, 4:5],
                                              ALU.subtract),
             reads=[K(4)], writes=[K(5)])
        b.op("ACT", lambda e: e.activation(stat[:, 6:7], stat[:, 5:6], AF.Ln,
                                           bias=consts[:, 0:1]),
             reads=[K(5), ("consts",)], writes=[K(6)])
        b.op("ACT", lambda e: e.activation(stat[:, 7:8], stat[:, 6:7], AF.Exp,
                                           scale=consts[:, 1:2]),
             reads=[K(6), ("consts",)], writes=[K(7)])

    b.op("POOL", lambda e: e.memset(ones_row_bf[:, :], 1.0), writes=[("ones_row_bf",)])
    b.op("POOL", lambda e: e.memset(ones_row_f32[:, :], 1.0), writes=[("ones_row_f32",)])
    b.op("POOL", lambda e: e.memset(ones_col_bf[:, :], 1.0), writes=[("ones_col_bf",)])
    b.op("POOL", lambda e: e.memset(consts[:, 0:1], EPS), writes=[("consts",)])
    b.op("POOL", lambda e: e.memset(consts[:, 1:2], -0.5),
         reads=[("consts",)], writes=[("consts",)])
    dma("SP", ident_bf_sb[:, :], ident_bf.ap(), [], [("ident_bf_sb",)], "d_const")
    dma("SP", ident_f32_sb[:, :], ident_f32.ap(), [], [("ident_f32_sb",)], "d_const")

    for i in range(4):
        sl = slice(i * (N_TOK // 4), (i + 1) * (N_TOK // 4))
        dma("POOL", x_bf.ap()[sl, :], x_in.ap()[sl, :], [], [("x_bf", i)], "g_xbf")
    dma("POOL", wdq_bf.ap()[0:DQ, :], W_d_q.ap(), [], [("wdq_bf",)], "g_wbf")
    dma("POOL", wdkv_bf.ap(), W_d_kv.ap(), [], [("wdkv_bf",)], "g_wbf")
    dma("POOL", wukv_bf.ap()[0:DKV, :],
        bass.AP(W_u_kv, 0, [[192 * DKV, 16], [DKV, 64], [1, DKV]]),
        [], [("wukv_bf",)], "g_wbf")
    dma("POOL", wukv_bf.ap()[DKV:3072, :],
        bass.AP(W_u_kv, 64 * DKV, [[192 * DKV, 16], [DKV, 128], [1, DKV]]),
        [], [("wukv_bf",)], "g_wbf")
    dma("POOL", wo_bf.ap(), W_o.ap(), [], [("wo_bf",)], "g_wbf")
    dma("POOL", gatew_bf.ap(), gate_w.ap(), [], [("gatew_bf",)], "g_wbf")

    def emit_moe_staging(ei, pace=()):
        if True:
            pace = list(pace)
            if ei < NE:
                dma("POOL", wgu_bf.ap()[ei * 2048:(ei + 1) * 2048, :],
                    w_gate_up.ap()[ei, :, :], pace, [("wgu_bf", ei)], "g_mw")
                dma("POOL", wdn_bf.ap()[ei * 1408:(ei + 1) * 1408, :],
                    w_down.ap()[ei, :, :], pace, [("wdn_bf", ei)], "g_mw")
            else:
                half = ei - NE
                dma("POOL", bass.AP(wgu_bf, ei * 2048 * GU, [[GU, 2048], [1, I_FF]]),
                    shared_gu.ap()[:, half * I_FF:(half + 1) * I_FF],
                    pace, [("wgu_bf", ei)], "g_mw")
                dma("POOL", bass.AP(wgu_bf, ei * 2048 * GU + I_FF, [[GU, 2048], [1, I_FF]]),
                    shared_gu.ap()[:, GU + half * I_FF:GU + (half + 1) * I_FF],
                    pace, [("wgu_bf", ei)], "g_mw")
                dma("POOL", wdn_bf.ap()[ei * 1408:(ei + 1) * 1408, :],
                    shared_dn.ap()[half * I_FF:(half + 1) * I_FF, :],
                    pace, [("wdn_bf", ei)], "g_mw")
    zinit = aalloc("zinit", [128, D], F32)
    b.op("POOL", lambda e: e.memset(zinit[:, :], 0.0), writes=[("zinit",)])
    dma("SP", bass.AP(idx_tab, 0, [[(NE * CAP + 128) // 128, 128], [1, (NE * CAP + 128) // 128]]),
        zinit[:, 0:(NE * CAP + 128) // 128],
        [("zinit",)], [("idx_tab",)], "d_it")
    zpad = aalloc("zpad", [6, D], BF16)
    b.op("POOL", lambda e: e.memset(zpad[:, :], 0.0), writes=[("zpad",)])
    dma("POOL", wdq_bf.ap()[DQ:DQ + 6, :], zpad[:, :], [("zpad",)], [("wdq_bf",)], "g_wbf")

    xt_tile = [aalloc(f"xt_tile{i}", [128, 512], BF16) for i in range(2)]
    for c in range(16):
        for rb in range(8):
            slot = (c * 8 + rb) % 2
            tl = xt_tile[slot]
            dma_t(tl[:, :], x_bf.ap()[rb * 512:(rb + 1) * 512, c * 128:(c + 1) * 128],
                  [("x_bf", rb // 2)], [("xt_tile", slot)], "d_xti")
            dma("SP", xT.ap()[c * 128:(c + 1) * 128, rb * 512:(rb + 1) * 512], tl[:, :],
                [("xt_tile", slot)], [("xT", slot)], "d_xT")
    if phases <= 0:
        b.emit(); ctx.close(); return nc

    wdqT = [aalloc(f"wdqT{c}", [128, 688], BF16) for c in range(16)]
    for c in range(16):
        dma_t(wdqT[c][:, :], wdq_bf.ap()[:, c * 128:(c + 1) * 128],
              [("wdq_bf",)], [("wdqT", c)], "d_wdqT")
    bdq_row = aalloc("bdq_row", [1, DQ], BF16)
    dma("POOL", bdq_row[:, :], row_ap(b_d_q, DQ), [], [("bdq_row",)], "g_bias")
    gq_row = aalloc("gq_row", [1, DQ], F32)
    btq_row = aalloc("btq_row", [1, DQ], F32)
    dma("SP", gq_row[:, :], row_ap(g_q, DQ), [], [("gq_row",)], "d_bias")
    dma("SP", btq_row[:, :], row_ap(bt_q, DQ), [], [("btq_row",)], "d_bias")
    gq_bc = aalloc("gq_bc", [128, DQ], F32)
    btq_bc = aalloc("btq_bc", [128, DQ], F32)
    for row, bc, rkey, wkey in [(gq_row, gq_bc, "gq_row", "gq_bc"),
                                (btq_row, btq_bc, "btq_row", "btq_bc")]:
        for g0 in range(2):
            n = 512 if g0 == 0 else DQ - 512
            ps = PS[2]
            b.op("PE", lambda e, row=row, g0=g0, n=n, ps=ps: e.matmul(
                ps[:, 0:n], ones_row_f32[:, :], row[:, g0 * 512:g0 * 512 + n],
                start=True, stop=True),
                reads=[("ones_row_f32",), (rkey,)], war_only=[("ps_bc1",)],
                writes=[("ps_bc1",)])
            b.op("DVE", lambda e, bc=bc, g0=g0, n=n, ps=ps: e.tensor_copy(
                bc[:, g0 * 512:g0 * 512 + n], ps[:, 0:n]),
                reads=[("ps_bc1",)], writes=[(wkey,)])

    cq_f32 = [aalloc(f"cqf{i}", [128, DQ], F32) for i in range(2)]
    cq_bf = [aalloc(f"cqbf{i}", [128, DQ_PAD], BF16) for i in range(2)]
    sqs1 = aalloc("sqs1", [128, DQ], F32)
    stat1 = [aalloc(f"stat1_{i}", [128, 8], F32) for i in range(2)]
    xt_blk1 = [aalloc(f"xtb1_{i}", [128, 16 * 512], BF16) for i in range(2)]
    for i in range(2):
        b.op("POOL", lambda e, i=i: e.memset(cq_bf[i][:, DQ:DQ_PAD], 0.0),
             writes=[("cq_bf", i)])
    for tc in range(N_TC):
        slot = (tc // 4) % 2
        t4 = tc % 4
        xtb = xt_blk1[slot]
        if t4 == 0:
            src = bass.AP(xT, (tc // 4) * 512, [[N_TOK, 128], [128 * N_TOK, 16], [1, 512]])
            dma("SP", xtb[:, :], src, [("xT", 0), ("xT", 1)], [("xt_blk1", slot)], "d_p1l")
        for g0 in range(2):
            n = 512 if g0 == 0 else DQ - 512
            ps = PS[g0]
            for kc in range(16):
                b.op("PE", lambda e, ps=ps, kc=kc, g0=g0, n=n, xtb=xtb, t4=t4: e.matmul(
                    ps[:, 0:n], xtb[:, kc * 512 + t4 * 128:kc * 512 + (t4 + 1) * 128],
                    wdqT[kc][:, g0 * 512:g0 * 512 + n],
                    start=(kc == 0), stop=False),
                    reads=[("xt_blk1", (tc // 4) % 2), ("wdqT", kc)],
                    war_only=[("ps_cq", g0)] if kc == 0 else (),
                    no_inc=True)
            b.op("PE", lambda e, ps=ps, g0=g0, n=n: e.matmul(
                ps[:, 0:n], ones_row_bf[:, :], bdq_row[:, g0 * 512:g0 * 512 + n],
                start=False, stop=True),
                reads=[("ones_row_bf",), ("bdq_row",)], writes=[("ps_cq", g0)])
            b.op("DVE", lambda e, ps=ps, g0=g0, n=n, lslot=tc % 2: e.tensor_copy(
                cq_f32[lslot][:, g0 * 512:g0 * 512 + n], ps[:, 0:n]),
                reads=[("ps_cq", g0)], writes=[("cqf", tc % 2)])
        slot = tc % 2
        cf = cq_f32[slot]
        st1 = stat1[slot]
        ln_stats(st1, sqs1, cf[:, :], DQ, [("cqf", slot)], ("s1", slot), ("sqs1",))
        b.op("DVE", lambda e, cf=cf, st1=st1: e.tensor_scalar(cf[:, :], cf[:, :], st1[:, 2:3],
                                                              st1[:, 7:8], ALU.subtract, ALU.mult),
             reads=[("cqf", slot), (("s1", slot), "stat")], writes=[("cqf", slot)])
        b.op("DVE", lambda e, cf=cf: e.tensor_tensor(cf[:, :], cf[:, :], gq_bc[:, :], ALU.mult),
             reads=[("cqf", slot), ("gq_bc",)], writes=[("cqf", slot)])
        b.op("DVE", lambda e, cf=cf, slot=slot: e.tensor_tensor(
            cq_bf[slot][:, 0:DQ], cf[:, :], btq_bc[:, :], ALU.add),
            reads=[("cqf", slot), ("btq_bc",)], writes=[("cq_bf", slot)])
        dma("SP", cq.ap()[tc * 128:(tc + 1) * 128, :], cq_bf[slot][:, :],
            [("cq_bf", slot)], [("cq", slot)], "d_cq")
    arena_reset()
    if phases <= 1:
        b.emit(); ctx.close(); return nc

    wuq_f32 = [aalloc(f"wuqf{c}", [128, D], F32) for c in range(6)]
    wuqT_bf = [aalloc(f"wuqbf{c}", [128, D], BF16) for c in range(6)]
    wuq_ld = aalloc("wuq_ld", [128, DQ], F32)
    cosA_sb = aalloc("cosA_sb", [128, 512], F32)
    sinA_sb = aalloc("sinA_sb", [128, 512], F32)
    dma("SP", cosA_sb[:, :], cosA_in.ap(), [], [("cosA",)], "d_const")
    dma("SP", sinA_sb[:, :], sinA_in.ap(), [], [("sinA",)], "d_const")
    for rc in range(16):
        dma("SP", wuq_ld[:, :], W_u_q.ap()[rc * 128:(rc + 1) * 128, :],
            [("wuq_ld",)], [("wuq_ld",)], "d_wuq")
        for c in range(6):
            n = 128 if c < 5 else DQ - 640
            ps = PS[0]
            b.op("PE", lambda e, c=c, n=n, ps=ps: e.transpose(
                ps[0:n, 0:128], wuq_ld[:, c * 128:c * 128 + n], ident_f32_sb[:, :]),
                reads=[("wuq_ld",), ("ident_f32_sb",)], war_only=[("ps_tr",)],
                writes=[("ps_tr",)])
            b.op("DVE", lambda e, c=c, n=n, rc=rc, ps=ps: e.tensor_copy(
                wuq_f32[c][0:n, rc * 128:(rc + 1) * 128], ps[0:n, 0:128]),
                reads=[("ps_tr",)], writes=[("wuqf", c)])
    for c in range(6):
        w = wuq_f32[c]
        kk = 128 if c < 5 else DQ - 640
        A = bass.AP(w, 64, [[D, kk], [128, 16], [1, 32]])
        Bp = bass.AP(w, 96, [[D, kk], [128, 16], [1, 32]])
        cosAP = bass.AP(cosA_sb, 0, [[512, kk], [32, 16], [1, 32]])
        sinAP = bass.AP(sinA_sb, 0, [[512, kk], [32, 16], [1, 32]])
        if c == 0:
            _rt = [aalloc(f"rt{i}", [128, 512], F32) for i in range(3)]
        t1, t2, t3 = _rt
        t1ap = bass.AP(t1, 0, [[512, kk], [32, 16], [1, 32]])
        t2ap = bass.AP(t2, 0, [[512, kk], [32, 16], [1, 32]])
        t3ap = bass.AP(t3, 0, [[512, kk], [32, 16], [1, 32]])
        b.op("DVE", lambda e, A=A, cosAP=cosAP, t1ap=t1ap: e.tensor_tensor(
            t1ap, A, cosAP, ALU.mult),
            reads=[("wuqf", c), ("cosA",)], writes=[("rt1",)])
        b.op("DVE", lambda e, Bp=Bp, sinAP=sinAP, t2ap=t2ap: e.tensor_tensor(
            t2ap, Bp, sinAP, ALU.mult),
            reads=[("wuqf", c), ("sinA",)], writes=[("rt2",)])
        b.op("DVE", lambda e, A=A, sinAP=sinAP, t3ap=t3ap: e.tensor_tensor(
            t3ap, A, sinAP, ALU.mult),
            reads=[("wuqf", c), ("sinA",)], writes=[("rt3",)])
        b.op("DVE", lambda e, Bp=Bp, cosAP=cosAP: e.tensor_tensor(
            Bp, Bp, cosAP, ALU.mult),
            reads=[("wuqf", c), ("cosA",), ("rt2",), ("rt3",)], writes=[("wuqf", c)])
        b.op("DVE", lambda e, Bp=Bp, t3ap=t3ap: e.tensor_tensor(
            Bp, Bp, t3ap, ALU.subtract),
            reads=[("wuqf", c), ("rt3",)], writes=[("wuqf", c)])
        b.op("DVE", lambda e, A=A, t1ap=t1ap, t2ap=t2ap: e.tensor_tensor(
            A, t1ap, t2ap, ALU.add),
            reads=[("rt1",), ("rt2",), ("wuqf", c)], writes=[("wuqf", c)])
        b.op("ACT", lambda e, c=c, kk=kk: e.activation(
            wuqT_bf[c][0:kk, :], wuq_f32[c][0:kk, :], AF.Copy),
            reads=[("wuqf", c)], writes=[("wuqbf", c)])

    buq_row = aalloc("buq_row", [1, D], F32)
    dma("SP", buq_row[:, :], row_ap(b_u_q, D), [], [("buq_row",)], "d_bias")
    bA = bass.AP(buq_row, 64, [[D, 1], [128, 16], [1, 32]])
    bB = bass.AP(buq_row, 96, [[D, 1], [128, 16], [1, 32]])
    cos1 = bass.AP(cosA_sb, 0, [[512, 1], [32, 16], [1, 32]])
    sin1 = bass.AP(sinA_sb, 0, [[512, 1], [32, 16], [1, 32]])
    bt1 = aalloc("bt1", [1, 512], F32)
    bt2 = aalloc("bt2", [1, 512], F32)
    bt3 = aalloc("bt3", [1, 512], F32)
    bt1ap = bass.AP(bt1, 0, [[512, 1], [32, 16], [1, 32]])
    bt2ap = bass.AP(bt2, 0, [[512, 1], [32, 16], [1, 32]])
    bt3ap = bass.AP(bt3, 0, [[512, 1], [32, 16], [1, 32]])
    b.op("DVE", lambda e: e.tensor_tensor(bt1ap, bA, cos1, ALU.mult),
         reads=[("buq_row",), ("cosA",)], writes=[("bt1",)])
    b.op("DVE", lambda e: e.tensor_tensor(bt2ap, bB, sin1, ALU.mult),
         reads=[("buq_row",), ("sinA",)], writes=[("bt2",)])
    b.op("DVE", lambda e: e.tensor_tensor(bt3ap, bA, sin1, ALU.mult),
         reads=[("buq_row",), ("sinA",)], writes=[("bt3",)])
    b.op("DVE", lambda e: e.tensor_tensor(bB, bB, cos1, ALU.mult),
         reads=[("buq_row",), ("cosA",), ("bt2",), ("bt3",)], writes=[("buq_row",)])
    b.op("DVE", lambda e: e.tensor_tensor(bB, bB, bt3ap, ALU.subtract),
         reads=[("buq_row",), ("bt3",)], writes=[("buq_row",)])
    b.op("DVE", lambda e: e.tensor_tensor(bA, bt1ap, bt2ap, ALU.add),
         reads=[("bt1",), ("bt2",), ("buq_row",)], writes=[("buq_row",)])
    dma("SP", row_ap(brot, D), buq_row[:, :], [("buq_row",)], [("brot",)], "d_bias")
    buq_col = aalloc("buq_col", [128, 16], F32)
    for m in range(16):
        dma("SP", buq_col[:, m:m + 1], bass.AP(brot, m * 128, [[1, 128], [1, 1]]),
            [("brot",)], [("buq_col",)], "d_bias")

    cqT_t = [[aalloc(f"cqT{s}_{c}", [128, 512], BF16) for c in range(6)] for s in range(2)]
    q_ev = [aalloc(f"qev{i}", [128, 512], BF16) for i in range(2)]
    for qt in range(N_QT):
        slot = qt % 2
        for c in range(6):
            dma_t(cqT_t[slot][c][:, :],
                  cq.ap()[qt * 512:(qt + 1) * 512, c * 128:(c + 1) * 128],
                  [("cq", 0), ("cq", 1)], [("cqT", slot)], "d_cqT")
        for m in range(16):
            ps = PS[m % 2]
            for c in range(6):
                kk = 128 if c < 5 else DQ - 640
                b.op("PE", lambda e, ps=ps, c=c, m=m, kk=kk, slot=slot: e.matmul(
                    ps[:, :], wuqT_bf[c][0:kk, m * 128:(m + 1) * 128],
                    cqT_t[slot][c][0:kk, :],
                    start=(c == 0), stop=(c == 5)),
                    reads=[("wuqbf", c), ("cqT", slot)],
                    war_only=[("ps_q", m % 2)] if c == 0 else (),
                    writes=[("ps_q", m % 2)] if c == 5 else (),
                    no_inc=(c != 5))
            ev = q_ev[m % 2]
            b.op("DVE", lambda e, ps=ps, ev=ev, m=m: e.tensor_scalar(
                ev[:, :], ps[:, :], buq_col[:, m:m + 1], None, ALU.add),
                reads=[("ps_q", m % 2), ("buq_col",)], writes=[("q_ev", m % 2)])
            dma("SP", qT.ap()[m * 128:(m + 1) * 128, qt * 512:(qt + 1) * 512], ev[:, :],
                [("q_ev", m % 2)], [("qT", m % 2)], "d_qT")
    arena_reset()
    if phases <= 2:
        b.emit(); ctx.close(); return nc

    wdkvT = [aalloc(f"wdkvT{c}", [128, DKV + DKR], BF16) for c in range(16)]
    for c in range(16):
        dma_t(wdkvT[c][:, :], wdkv_bf.ap()[:, c * 128:(c + 1) * 128],
              [("wdkv_bf",)], [("wdkvT", c)], "d_wdkvT")
    bdkv_row = aalloc("bdkv_row", [1, DKV + DKR], BF16)
    dma("POOL", bdkv_row[:, :], row_ap(b_d_kv, DKV + DKR), [], [("bdkv_row",)], "g_bias")
    gkv_row = aalloc("gkv_row", [1, DKV], F32)
    btkv_row = aalloc("btkv_row", [1, DKV], F32)
    dma("SP", gkv_row[:, :], row_ap(g_kv, DKV), [], [("gkv_row",)], "d_bias")
    dma("SP", btkv_row[:, :], row_ap(bt_kv, DKV), [], [("btkv_row",)], "d_bias")
    gkv_bc = aalloc("gkv_bc", [128, DKV], F32)
    btkv_bc = aalloc("btkv_bc", [128, DKV], F32)
    for row, bc, rkey, wkey in [(gkv_row, gkv_bc, "gkv_row", "gkv_bc"),
                                (btkv_row, btkv_bc, "btkv_row", "btkv_bc")]:
        for g0 in range(2):
            ps = PS[5]
            b.op("PE", lambda e, row=row, g0=g0, ps=ps: e.matmul(
                ps[:, :], ones_row_f32[:, :], row[:, g0 * 512:(g0 + 1) * 512],
                start=True, stop=True),
                reads=[("ones_row_f32",), (rkey,)], war_only=[("ps_bc3",)],
                writes=[("ps_bc3",)])
            b.op("DVE", lambda e, bc=bc, g0=g0, ps=ps: e.tensor_copy(
                bc[:, g0 * 512:(g0 + 1) * 512], ps[:, :]),
                reads=[("ps_bc3",)], writes=[(wkey,)])

    ckv_f32 = [aalloc(f"ckvf{i}", [128, DKV + DKR], F32) for i in range(2)]
    kvn_bf = [aalloc(f"kvnbf{i}", [128, DKV], BF16) for i in range(2)]
    kr_bf = [aalloc(f"krbf{i}", [128, DKR], BF16) for i in range(2)]
    krT_sb = [aalloc(f"krTsb{i}", [64, 128], BF16) for i in range(2)]
    sqs3 = aalloc("sqs3", [128, DKV], F32)
    stat3 = [aalloc(f"stat3_{i}", [128, 8], F32) for i in range(2)]
    xt_blk3 = [aalloc(f"xtb3_{i}", [128, 16 * 512], BF16) for i in range(2)]
    groups3 = [(0, 512), (512, 512), (1024, 64)]
    for tc in range(N_TC):
        bslot = (tc // 4) % 2
        t4 = tc % 4
        xtb = xt_blk3[bslot]
        if t4 == 0:
            src = bass.AP(xT, (tc // 4) * 512, [[N_TOK, 128], [128 * N_TOK, 16], [1, 512]])
            dma("SP", xtb[:, :], src, [("xT", 0), ("xT", 1)], [("xt_blk3", bslot)], "d_p3l")
        slot = tc % 2
        for g0, (c0, n) in enumerate(groups3):
            ps = PS[2 + g0]
            for kc in range(16):
                b.op("PE", lambda e, ps=ps, kc=kc, c0=c0, n=n, xtb=xtb, t4=t4: e.matmul(
                    ps[:, 0:n], xtb[:, kc * 512 + t4 * 128:kc * 512 + (t4 + 1) * 128],
                    wdkvT[kc][:, c0:c0 + n],
                    start=(kc == 0), stop=False),
                    reads=[("xt_blk3", bslot), ("wdkvT", kc)],
                    war_only=[("ps_ckv", g0)] if kc == 0 else (),
                    no_inc=True)
            b.op("PE", lambda e, ps=ps, c0=c0, n=n: e.matmul(
                ps[:, 0:n], ones_row_bf[:, :], bdkv_row[:, c0:c0 + n],
                start=False, stop=True),
                reads=[("ones_row_bf",), ("bdkv_row",)], writes=[("ps_ckv", g0)])
            b.op("DVE", lambda e, ps=ps, c0=c0, n=n, slot=slot: e.tensor_copy(
                ckv_f32[slot][:, c0:c0 + n], ps[:, 0:n]),
                reads=[("ps_ckv", g0)], writes=[("ckvf", slot)])
        cf = ckv_f32[slot]
        st3 = stat3[slot]
        ln_stats(st3, sqs3, cf[:, 0:DKV], DKV, [("ckvf", slot)], ("s3", slot), ("sqs3",))
        b.op("ACT", lambda e, cf=cf, slot=slot: e.activation(
            kr_bf[slot][:, :], cf[:, DKV:DKV + DKR], AF.Copy),
            reads=[("ckvf", slot)], writes=[("kr_bf", slot)])
        b.op("DVE", lambda e, cf=cf, st3=st3: e.tensor_scalar(
            cf[:, 0:DKV], cf[:, 0:DKV], st3[:, 2:3], st3[:, 7:8],
            ALU.subtract, ALU.mult),
            reads=[("ckvf", slot), (("s3", slot), "stat")], writes=[("ckvf", slot)])
        b.op("DVE", lambda e, cf=cf: e.tensor_tensor(
            cf[:, 0:DKV], cf[:, 0:DKV], gkv_bc[:, :], ALU.mult),
            reads=[("ckvf", slot), ("gkv_bc",)], writes=[("ckvf", slot)])
        b.op("DVE", lambda e, cf=cf, slot=slot: e.tensor_tensor(
            kvn_bf[slot][:, :], cf[:, 0:DKV], btkv_bc[:, :], ALU.add),
            reads=[("ckvf", slot), ("btkv_bc",)], writes=[("kvn_bf", slot)])
        dma("SP", kvn.ap()[tc * 128:(tc + 1) * 128, :], kvn_bf[slot][:, :],
            [("kvn_bf", slot)], [("kvn", slot)], "d_kvn")
        b.op("PE", lambda e, slot=slot: e.transpose(
            PSB[0:64, 0:128], kr_bf[slot][:, :], ident_bf_sb[:, :]),
            reads=[("kr_bf", slot), ("ident_bf_sb",)], war_only=[("ps_krT",)],
            writes=[("ps_krT",)])
        b.op("DVE", lambda e, slot=slot: e.tensor_copy(krT_sb[slot][:, :], PSB[0:64, 0:128]),
             reads=[("ps_krT",)], writes=[("krT_sb", slot)])
        dma("SP", krT.ap()[:, tc * 128:(tc + 1) * 128], krT_sb[slot][:, :],
            [("krT_sb", slot)], [("krT", slot)], "d_krT")
    arena_reset()
    if phases <= 3:
        b.emit(); ctx.close(); return nc

    wukvT = [aalloc(f"wukvT{c}", [128, 3072], BF16) for c in range(8)]
    for c in range(8):
        for s4 in range(6):
            dma_t(wukvT[c][:, s4 * 512:(s4 + 1) * 512],
                  wukv_bf.ap()[s4 * 512:(s4 + 1) * 512, c * 128:(c + 1) * 128],
                  [("wukv_bf",)], [("wukvT", c)], "d_wukvT")
    bkn_col = aalloc("bkn_col", [128, 8], F32)
    for m in range(8):
        dma("SP", bkn_col[:, m:m + 1], bass.AP(b_u_kv, m * 2 * 192, [[192, 2], [1, 64]]),
            [("bkn_col",)], [("bkn_col",)], "d_bias")
    bv_row = aalloc("bv_row", [1, D], BF16)
    dma("POOL", bv_row[:, :], bass.AP(b_u_kv, 64, [[3072, 1], [192, 16], [1, 128]]),
        [], [("bv_row",)], "g_bias")

    kvnT_t = [[aalloc(f"kvnT{s}_{c}", [128, 512], BF16) for c in range(8)] for s in range(2)]
    kn_ev = [aalloc(f"knev{i}", [128, 512], BF16) for i in range(2)]
    for qt in range(N_QT):
        emit_moe_staging(qt, pace=[("kn_ev", qt % 2)] if qt >= 2 else [("kvnT", qt % 2)])
        slot = qt % 2
        for c in range(8):
            dma_t(kvnT_t[slot][c][:, :],
                  kvn.ap()[qt * 512:(qt + 1) * 512, c * 128:(c + 1) * 128],
                  [("kvn", 0), ("kvn", 1)], [("kvnT", slot)], "d_kvnT")
        for m in range(8):
            ps = PS[m % 2]
            for c in range(8):
                lh = wukvT[c][:, m * 128:(m + 1) * 128]
                b.op("PE", lambda e, ps=ps, c=c, lh=lh, slot=slot: e.matmul(
                    ps[:, :], lh, kvnT_t[slot][c][:, :],
                    start=(c == 0), stop=(c == 7)),
                    reads=[("wukvT", c), ("kvnT", slot)],
                    war_only=[("ps_kn", m % 2)] if c == 0 else (),
                    writes=[("ps_kn", m % 2)] if c == 7 else (),
                    no_inc=(c != 7))
            ev = kn_ev[m % 2]
            b.op("DVE", lambda e, ps=ps, ev=ev, m=m: e.tensor_scalar(
                ev[:, :], ps[:, :], bkn_col[:, m:m + 1], None, ALU.add),
                reads=[("ps_kn", m % 2), ("bkn_col",)], writes=[("kn_ev", m % 2)])
            dma("SP", knT.ap()[m * 128:(m + 1) * 128, qt * 512:(qt + 1) * 512], ev[:, :],
                [("kn_ev", m % 2)], [("knT", m % 2)], "d_knT")

    kvn_sta = [aalloc(f"kvnsta{i}", [128, 8 * 128], BF16) for i in range(2)]
    v_ev = [aalloc(f"vev{i}", [128, D], BF16) for i in range(2)]
    for tc in range(N_TC):
        if tc % 4 == 0:
            emit_moe_staging(8 + tc // 4, pace=[("v_ev", tc % 2)] if tc >= 2 else ())
        slot = tc % 2
        sta = kvn_sta[slot]
        for c in range(8):
            dma_t(sta[:, c * 128:(c + 1) * 128],
                  kvn.ap()[tc * 128:(tc + 1) * 128, c * 128:(c + 1) * 128],
                  [("kvn", 0), ("kvn", 1)], [("kvn_sta", slot)], "d_vl")
        for jg in range(4):
            ps = PS[2 + jg % 2]
            for c in range(8):
                rh = wukvT[c][:, DKV + jg * 512:DKV + (jg + 1) * 512]
                b.op("PE", lambda e, ps=ps, rh=rh, c=c, slot=slot, sta=sta: e.matmul(
                    ps[:, :], sta[:, c * 128:(c + 1) * 128], rh,
                    start=(c == 0), stop=False),
                    reads=[("kvn_sta", slot), ("wukvT", c)],
                    war_only=[("ps_v", jg % 2)] if c == 0 else (),
                    no_inc=True)
            b.op("PE", lambda e, ps=ps, jg=jg: e.matmul(
                ps[:, :], ones_row_bf[:, :], bv_row[:, jg * 512:(jg + 1) * 512],
                start=False, stop=True),
                reads=[("ones_row_bf",), ("bv_row",)], writes=[("ps_v", jg % 2)])
            b.op("DVE", lambda e, ps=ps, jg=jg, slot=slot: e.tensor_copy(
                v_ev[slot][:, jg * 512:(jg + 1) * 512], ps[:, :]),
                reads=[("ps_v", jg % 2)], writes=[("v_ev", slot)])
        dma("SP", v_sc.ap()[tc * 128:(tc + 1) * 128, :], v_ev[slot][:, :],
            [("v_ev", slot)], [("v_sc", slot)], "d_v")
    if phases <= 5:
        b.emit(); ctx.close(); return nc

    masks_sb = aalloc("masks_sb", [128, 5 * 512], F32)
    for j in range(5):
        dma("SP", masks_sb[:, j * 512:(j + 1) * 512],
            masks_in.ap()[j * 128:(j + 1) * 128, :],
            [("masks",)], [("masks",)], "d_const")
    q_blk = [aalloc(f"qblk{i}", [128, T], BF16) for i in range(2)]
    k_blk = [aalloc(f"kblk{i}", [128, T], BF16) for i in range(2)]
    v_seq = aalloc("v_seq", [128, 16 * 2048], BF16)
    sT_sb = [aalloc(f"sTsb{i}", [128, 512], F32) for i in range(4)]
    pT_sb = [aalloc(f"pTsb{i}", [128, 512], BF16) for i in range(6)]
    rcp_sb = [aalloc(f"rcp_sb{i}", [1, 512], F32) for i in range(2)]
    rcpb_sb = [aalloc(f"rcpb_sb{i}", [128, 512], F32) for i in range(2)]
    yn_ev = [aalloc(f"yn_ev{i}", [128, 512], BF16) for i in range(2)]
    vkeys = [("v_sc", 0), ("v_sc", 1)]

    for s in range(B_PER_CORE):
        for kc in range(16):
            dma("SP", v_seq[:, kc * 2048:(kc + 1) * 2048],
                v_sc.ap()[s * T + kc * 128:s * T + (kc + 1) * 128, :],
                vkeys + [("v_seq",)], [("v_seq",)], "d_av")
        for h in range(NH):
            if s == 0 and h < 2:
                emit_moe_staging(16 + h, pace=[("yn_ev", h % 2)])
            slot = (s * NH + h) % 2
            qb, kb = q_blk[slot], k_blk[slot]
            vb = None
            dma("SP", qb[:, :], qT.ap()[h * 128:(h + 1) * 128, s * T:(s + 1) * T],
                [("qT", 0), ("qT", 1)], [("q_blk", slot)], "d_aq")
            dma("SP", kb[0:64, :], knT.ap()[h * 64:(h + 1) * 64, s * T:(s + 1) * T],
                [("knT", 0), ("knT", 1)], [("k_blk", slot)], "d_ak")
            dma("SP", kb[64:128, :], krT.ap()[:, s * T:(s + 1) * T],
                [("krT", 0), ("krT", 1)], [("k_blk", slot)], "d_ak")
            tiles = []
            for qt in range(4):
                for kc in range(4 * qt + 4):
                    tiles.append((qt, kc))
            NT = len(tiles)
            LA = 4
            SB6 = [PS[0], PS[1], PS[5]]

            def emit_front(i):
                qt, kc = tiles[i]
                ps_s = SB6[i % 3]
                mj = kc - 4 * qt if kc >= 4 * qt else 4
                b.op("PE", lambda e, ps_s=ps_s, kc=kc, qt=qt, qb=qb, kb=kb: e.matmul(
                    ps_s[:, :], kb[:, kc * 128:(kc + 1) * 128],
                    qb[:, qt * 512:(qt + 1) * 512], start=True, stop=True),
                    reads=[("k_blk", slot), ("q_blk", slot)],
                    war_only=[("ps_s", i % 3)],
                    writes=[("ps_s", i % 3)])
                st = sT_sb[i % 4]
                b.op("DVE", lambda e, ps_s=ps_s, st=st, mj=mj: e.scalar_tensor_tensor(
                    st[:, :], ps_s[:, :], SCALE, masks_sb[:, mj * 512:(mj + 1) * 512],
                    ALU.mult, ALU.add),
                    reads=[("ps_s", i % 3), ("masks",)], writes=[("sT_sb", i % 4)])
                pt = pT_sb[i % 6]
                b.op("ACT", lambda e, st=st, pt=pt: e.activation(pt[:, :], st[:, :], AF.Exp),
                     reads=[("sT_sb", i % 4)], writes=[("pT_sb", i % 6)])

            def emit_back(j):
                qt, kc = tiles[j]
                nkc = 4 * qt + 4
                ps_y = PS[2 + qt % 2]
                ps_sum = PS[4]
                pt = pT_sb[j % 6]
                b.op("PE", lambda e, ps_y=ps_y, pt=pt, kc=kc, h=h, nkc=nkc: e.matmul(
                    ps_y[:, :], v_seq[:, kc * 2048 + h * 128:kc * 2048 + (h + 1) * 128],
                    pt[:, :],
                    start=(kc == 0), stop=(kc == nkc - 1)),
                    reads=[("pT_sb", j % 6), ("v_seq",)],
                    war_only=[("ps_y", qt % 2)] if kc == 0 else (),
                    writes=[("ps_y", qt % 2)] if kc == nkc - 1 else (),
                    no_inc=(kc != nkc - 1))
                b.op("PE", lambda e, ps_sum=ps_sum, pt=pt, kc=kc, nkc=nkc: e.matmul(
                    ps_sum[0:1, :], ones_col_bf[:, :], pt[:, :],
                    start=(kc == 0), stop=(kc == nkc - 1)),
                    reads=[("pT_sb", j % 6), ("ones_col_bf",)],
                    war_only=[("ps_sum",), ("ps_abc",)] if kc == 0 else (),
                    writes=[("ps_sum",)] if kc == nkc - 1 else (),
                    no_inc=(kc != nkc - 1))
                if kc != nkc - 1:
                    return
                qs = qt % 2
                b.op("DVE", lambda e, ps_sum=ps_sum, qs=qs: e.reciprocal(
                    rcp_sb[qs][:, :], ps_sum[0:1, :]),
                    reads=[("ps_sum",)], writes=[("rcp_sb", qs)])
                ps_bc = PS[4]
                b.op("PE", lambda e, ps_bc=ps_bc, qs=qs: e.matmul(
                    ps_bc[:, :], ones_row_f32[:, :], rcp_sb[qs][:, :], start=True, stop=True),
                    reads=[("ones_row_f32",), ("rcp_sb", qs)],
                    war_only=[("ps_abc",), ("ps_sum",)],
                    writes=[("ps_abc",)])
                b.op("DVE", lambda e, ps_bc=ps_bc, qs=qs: e.tensor_copy(
                    rcpb_sb[qs][:, :], ps_bc[:, :]),
                    reads=[("ps_abc",)], writes=[("rcpb_sb", qs)])
                b.op("DVE", lambda e, ps_y=ps_y, qs=qs: e.tensor_tensor(
                    yn_ev[qs][:, :], ps_y[:, :], rcpb_sb[qs][:, :], ALU.mult),
                    reads=[("ps_y", qt % 2), ("rcpb_sb", qs)], writes=[("yn_ev", qs)])
                dma("SP", ynT.ap()[h * 128:(h + 1) * 128,
                                   s * T + qt * 512:s * T + (qt + 1) * 512],
                    yn_ev[qs][:, :], [("yn_ev", qs)], [("ynT", qs)], "d_ynT")

            for i in range(NT + LA):
                if i < NT:
                    emit_front(i)
                j = i - LA
                if j >= 0:
                    emit_back(j)
    arena_reset()
    if phases <= 6:
        b.emit(); ctx.close(); return nc

    woT = [aalloc(f"woT{c}", [128, D], BF16) for c in range(16)]
    for c in range(16):
        for s4 in range(4):
            dma_t(woT[c][:, s4 * 512:(s4 + 1) * 512],
                  wo_bf.ap()[s4 * 512:(s4 + 1) * 512, c * 128:(c + 1) * 128],
                  [("wo_bf",)], [("woT", c)], "d_woT")
    bo_row = aalloc("bo_row", [1, D], BF16)
    dma("POOL", bo_row[:, :], row_ap(b_o, D), [], [("bo_row",)], "g_bias")
    yn_sta = [aalloc(f"ynsta{i}", [128, 16 * 512], BF16) for i in range(2)]
    x_chunk = [aalloc(f"xchunk{i}", [128, D], F32) for i in range(4)]
    r1 = [aalloc(f"r1_{i}", [128, D], F32) for i in range(4)]
    x1bf_t = [aalloc(f"x1bft{i}", [128, D], BF16) for i in range(4)]
    sqs7 = aalloc("sqs7", [128, D], F32)
    stat7 = [aalloc(f"stat7_{i}", [128, 8], F32) for i in range(4)]
    for tc in range(N_TC):
        bslot = (tc // 4) % 2
        t4 = tc % 4
        slot = tc % 4
        sta = yn_sta[bslot]
        if t4 == 0:
            src = bass.AP(ynT, (tc // 4) * 512, [[N_TOK, 128], [128 * N_TOK, 16], [1, 512]])
            dma("SP", sta[:, :], src, [("ynT", 0), ("ynT", 1)], [("yn_sta", bslot)], "d_p7l")
        dma("SP", x_chunk[slot][:, :], x_in.ap()[tc * 128:(tc + 1) * 128, :],
            [("x_chunk", slot)], [("x_chunk", slot)], "d_p7x")
        for jg in range(4):
            ps = PS[jg % 2]
            for c in range(16):
                b.op("PE", lambda e, ps=ps, c=c, jg=jg, sta=sta, t4=t4: e.matmul(
                    ps[:, :], sta[:, c * 512 + t4 * 128:c * 512 + (t4 + 1) * 128],
                    woT[c][:, jg * 512:(jg + 1) * 512],
                    start=(c == 0), stop=False),
                    reads=[("yn_sta", bslot), ("woT", c)],
                    war_only=[("ps_wo", jg % 2)] if c == 0 else (),
                    no_inc=True)
            b.op("PE", lambda e, ps=ps, jg=jg: e.matmul(
                ps[:, :], ones_row_bf[:, :], bo_row[:, jg * 512:(jg + 1) * 512],
                start=False, stop=True),
                reads=[("ones_row_bf",), ("bo_row",)], writes=[("ps_wo", jg % 2)])
            b.op("DVE", lambda e, ps=ps, jg=jg, slot=slot: e.tensor_tensor(
                r1[slot][:, jg * 512:(jg + 1) * 512], ps[:, :],
                x_chunk[slot][:, jg * 512:(jg + 1) * 512], ALU.add),
                reads=[("ps_wo", jg % 2), ("x_chunk", slot)], writes=[("r1", slot)])
        rf = r1[slot]
        st7 = stat7[slot]
        ln_stats(st7, sqs7, rf[:, :], D, [("r1", slot)], ("s7", slot), ("sqs7",))
        b.op("DVE", lambda e, rf=rf, st7=st7: e.tensor_scalar(rf[:, :], rf[:, :], st7[:, 2:3],
                                                              st7[:, 7:8], ALU.subtract, ALU.mult),
             reads=[("r1", slot), (("s7", slot), "stat")], writes=[("r1", slot)])
        b.op("ACT", lambda e, rf=rf, slot=slot: e.activation(
            x1bf_t[slot][:, :], rf[:, :], AF.Copy),
            reads=[("r1", slot)], writes=[("x1bf_t", slot)])
        dma("SP", x1f.ap()[tc * 128:(tc + 1) * 128, :], rf[:, :],
            [("r1", slot)], [("x1f", slot)], "d_x1f")
        dma("POOL", yacc.ap()[tc * 128:(tc + 1) * 128, :], rf[:, :],
            [("r1", slot)], [("yacc", tc)], "g_yacc0")
        dma("SP", x1bf.ap()[tc * 128:(tc + 1) * 128, :], x1bf_t[slot][:, :],
            [("x1bf_t", slot)], [("x1bf", slot)], "d_x1bf")

    x1t_tile = [aalloc(f"x1t{i}", [128, 512], BF16) for i in range(2)]
    for c in range(16):
        for rb in range(8):
            slot = (c * 8 + rb) % 2
            tl = x1t_tile[slot]
            dma_t(tl[:, :], x1bf.ap()[rb * 512:(rb + 1) * 512, c * 128:(c + 1) * 128],
                  [("x1bf", i) for i in range(4)], [("x1t_tile", slot)], "d_x1ti")
            dma("SP", x1T.ap()[c * 128:(c + 1) * 128, rb * 512:(rb + 1) * 512], tl[:, :],
                [("x1t_tile", slot)], [("x1T", slot)], "d_x1T")
    arena_reset()
    if phases <= 8:
        b.emit(); ctx.close(); return nc

    gwT = aalloc("gwT", [128, 16 * 16], BF16)
    for c in range(16):
        dma_t(gwT[:, c * 16:(c + 1) * 16], gatew_bf.ap()[:, c * 128:(c + 1) * 128],
              [("gatew_bf",)], [("gwT",)], "d_gwT")
    iota_sb = aalloc("iota_sb", [128, 16], F32)
    dma("SP", iota_sb[:, :], iota16_in.ap(), [], [("iota_sb",)], "d_const")
    x1t_blk = [aalloc(f"x1tb{i}", [128, 16 * 512], BF16) for i in range(2)]
    glog = aalloc("glog", [128, 16], F32)
    gsc = aalloc("gsc", [128, 16], F32)
    gstat = aalloc("gstat", [128, 6], F32)
    gmask = aalloc("gmask", [128, 16], U8)
    gmask2 = aalloc("gmask2", [128, 16], U8)
    gtmp = aalloc("gtmp", [128, 16], F32)
    gbig = aalloc("gbig", [128, 16], F32)
    b.op("POOL", lambda e: e.memset(gbig[:, :], 1e9), writes=[("gbig",)])
    tri_sb = aalloc("tri_sb", [128, 128], F32)
    dma("SP", tri_sb[:, :], tri_in.ap(), [], [("tri_sb",)], "d_tri")
    iotacol_sb = aalloc("iotacol_sb", [128, 1], F32)
    dma("SP", iotacol_sb[:, :], iotacol_in.ap(), [], [("iotacol_sb",)], "d_ic")
    p9ones = aalloc("p9ones", [128, 1], F32)
    b.op("POOL", lambda e: e.memset(p9ones[:, :], 1.0), writes=[("p9ones",)])
    prefix_row = aalloc("prefix_row", [1, 16], F32)
    b.op("POOL", lambda e: e.memset(prefix_row[:, :], 0.0), writes=[("prefix_row",)])
    gm1f = [aalloc(f"gm1f{i}", [128, 16], F32) for i in range(2)]
    gm2f = [aalloc(f"gm2f{i}", [128, 16], F32) for i in range(2)]
    gsel = [aalloc(f"gsel{i}", [128, 16], F32) for i in range(2)]
    gcum = [aalloc(f"gcum{i}", [128, 16], F32) for i in range(2)]
    gsc2s = [aalloc(f"gsc2s{i}", [128, 16], F32) for i in range(2)]
    gp = [aalloc(f"gp{i}", [128, 8], F32) for i in range(2)]
    gw = [aalloc(f"gw{i}", [128, 8], F32) for i in range(2)]
    goi = [aalloc(f"goi{i}", [128, 6], mybir.dt.int32) for i in range(2)]
    for tc in range(N_TC):
        bslot = (tc // 4) % 2
        t4 = tc % 4
        slot = tc % 2
        xtb = x1t_blk[bslot]
        if t4 == 0:
            src = bass.AP(x1T, (tc // 4) * 512, [[N_TOK, 128], [128 * N_TOK, 16], [1, 512]])
            dma("SP", xtb[:, :], src, [("x1T", 0), ("x1T", 1)], [("x1t_blk", bslot)], "d_p9l")
        ps = PS[0]
        for kc in range(16):
            b.op("PE", lambda e, ps=ps, kc=kc, xtb=xtb, t4=t4: e.matmul(
                ps[:, 0:16], xtb[:, kc * 512 + t4 * 128:kc * 512 + (t4 + 1) * 128],
                gwT[:, kc * 16:(kc + 1) * 16],
                start=(kc == 0), stop=(kc == 15)),
                reads=[("x1t_blk", bslot), ("gwT",)],
                war_only=[("ps_g",)] if kc == 0 else (),
                writes=[("ps_g",)] if kc == 15 else (),
                no_inc=(kc != 15))
        b.op("DVE", lambda e, ps=ps: e.tensor_copy(glog[:, :], ps[:, 0:16]),
             reads=[("ps_g",)], writes=[("glog",)])
        b.op("DVE", lambda e: e.reduce_max(gstat[:, 0:1], glog[:, :], axis=AX.X),
             reads=[("glog",)], writes=[("gstat",)])
        b.op("DVE", lambda e: e.tensor_scalar(glog[:, :], glog[:, :], gstat[:, 0:1], None,
                                              ALU.subtract),
             reads=[("glog",), ("gstat",)], writes=[("glog",)])
        b.op("ACT", lambda e: e.activation(gsc[:, :], glog[:, :], AF.Exp,
                                           accum_out=gstat[:, 1:2]),
             reads=[("glog",), ("gstat",)], writes=[("gsc",), ("gstat",)])
        b.op("DVE", lambda e: e.reciprocal(gstat[:, 2:3], gstat[:, 1:2]),
             reads=[("gstat",)], writes=[("gstat",)])
        b.op("DVE", lambda e: e.tensor_scalar(gsc[:, :], gsc[:, :], gstat[:, 2:3], None,
                                              ALU.mult),
             reads=[("gsc",), ("gstat",)], writes=[("gsc",)])
        b.op("DVE", lambda e: e.reduce_max(gstat[:, 3:4], gsc[:, :], axis=AX.X),
             reads=[("gsc",)], writes=[("gstat",)])
        b.op("DVE", lambda e: e.tensor_scalar(gmask[:, :], gsc[:, :], gstat[:, 3:4], None,
                                              ALU.is_equal),
             reads=[("gsc",), ("gstat",)], writes=[("gmask",)])
        b.op("DVE", lambda e: e.select(gtmp[:, :], gmask[:, :], iota_sb[:, :], gbig[:, :], add_drain=True),
             reads=[("gmask",), ("iota_sb",), ("gbig",)], writes=[("gtmp",)])
        b.op("DVE", lambda e: e.tensor_reduce(gstat[:, 4:5], gtmp[:, :], AX.X, ALU.min),
             reads=[("gtmp",)], writes=[("gstat",)])
        b.op("DVE", lambda e: e.tensor_scalar(gmask[:, :], iota_sb[:, :], gstat[:, 4:5], None,
                                              ALU.is_equal),
             reads=[("iota_sb",), ("gstat",)], writes=[("gmask",)])
        b.op("DVE", lambda e: e.tensor_copy(gtmp[:, :], gmask[:, :]),
             reads=[("gmask",)], writes=[("gtmp",)])
        b.op("DVE", lambda e: e.scalar_tensor_tensor(gtmp[:, :], gtmp[:, :], -2e9, gsc[:, :],
                                                     ALU.mult, ALU.add),
             reads=[("gtmp",), ("gsc",)], writes=[("gtmp",)])
        b.op("DVE", lambda e: e.reduce_max(gstat[:, 5:6], gtmp[:, :], axis=AX.X),
             reads=[("gtmp",)], writes=[("gstat",)])
        b.op("DVE", lambda e: e.tensor_scalar(gmask2[:, :], gtmp[:, :], gstat[:, 5:6], None,
                                              ALU.is_equal),
             reads=[("gtmp",), ("gstat",)], writes=[("gmask2",)])
        b.op("DVE", lambda e: e.select(gtmp[:, :], gmask2[:, :], iota_sb[:, :], gbig[:, :], add_drain=True),
             reads=[("gmask2",), ("iota_sb",), ("gbig",)], writes=[("gtmp",)])
        b.op("DVE", lambda e: e.tensor_reduce(gstat[:, 4:5], gtmp[:, :], AX.X, ALU.min),
             reads=[("gtmp",)], writes=[("gstat",)])
        b.op("DVE", lambda e: e.tensor_scalar(gmask2[:, :], iota_sb[:, :], gstat[:, 4:5], None,
                                              ALU.is_equal),
             reads=[("iota_sb",), ("gstat",)], writes=[("gmask2",)])
        m1, m2, sl, cm, sc2, p8, w8, oi = (gm1f[slot], gm2f[slot], gsel[slot],
                                           gcum[slot], gsc2s[slot], gp[slot],
                                           gw[slot], goi[slot])
        b.op("DVE", lambda e, m1=m1: e.tensor_copy(m1[:, :], gmask[:, :]),
             reads=[("gmask",)], writes=[("gm1f", slot)])
        b.op("DVE", lambda e, m2=m2: e.tensor_copy(m2[:, :], gmask2[:, :]),
             reads=[("gmask2",)], writes=[("gm2f", slot)])
        b.op("DVE", lambda e, sc2=sc2: e.tensor_copy(sc2[:, :], gsc[:, :]),
             reads=[("gsc",)], writes=[("gsc2s", slot)])
        b.op("DVE", lambda e, sl=sl, m1=m1, m2=m2: e.tensor_tensor(
            sl[:, :], m1[:, :], m2[:, :], ALU.add),
            reads=[("gm1f", slot), ("gm2f", slot)], writes=[("gsel", slot)])
        psC = PS[1]
        psT = PS[2]
        b.op("PE", lambda e, psC=psC, sl=sl: e.matmul(
            psC[:, 0:16], tri_sb[:, :], sl[:, :], start=True, stop=False),
            reads=[("tri_sb",), ("gsel", slot)], war_only=[("ps_cum",)], no_inc=True)
        b.op("PE", lambda e, psC=psC: e.matmul(
            psC[:, 0:16], ones_row_f32[:, :], prefix_row[:, :], start=False, stop=True),
            reads=[("ones_row_f32",), ("prefix_row",)], writes=[("ps_cum",)])
        b.op("PE", lambda e, psT=psT, sl=sl: e.matmul(
            psT[0:1, 0:16], p9ones[:, :], sl[:, :], start=True, stop=True),
            reads=[("p9ones",), ("gsel", slot)], war_only=[("ps_tot",)],
            writes=[("ps_tot",)])
        b.op("DVE", lambda e, cm=cm, psC=psC: e.tensor_copy(cm[:, :], psC[:, 0:16]),
             reads=[("ps_cum",)], writes=[("gcum", slot)])
        b.op("DVE", lambda e, psT=psT: e.tensor_tensor(
            prefix_row[:, :], prefix_row[:, :], psT[0:1, 0:16], ALU.add),
            reads=[("ps_tot",), ("prefix_row",)], writes=[("prefix_row",)])
        for ri, mk in ((0, m1), (1, m2)):
            b.op("DVE", lambda e, mk=mk, cm=cm: e.tensor_tensor(
                gtmp[:, :], cm[:, :], mk[:, :], ALU.mult),
                reads=[("gcum", slot), ("gm1f", slot), ("gm2f", slot)],
                writes=[("gtmp",)])
            b.op("DVE", lambda e, p8=p8, ri=ri: e.reduce_sum(
                p8[:, ri:ri + 1], gtmp[:, :], axis=AX.X),
                reads=[("gtmp",)], writes=[("gp", slot, "p", ri)])
            b.op("DVE", lambda e, mk=mk: e.tensor_tensor(
                gtmp[:, :], iota_sb[:, :], mk[:, :], ALU.mult),
                reads=[("iota_sb",), ("gm1f", slot), ("gm2f", slot), ("gp", slot, "p", ri)],
                writes=[("gtmp",)])
            b.op("DVE", lambda e, p8=p8, ri=ri: e.reduce_sum(
                p8[:, 2 + ri:3 + ri], gtmp[:, :], axis=AX.X),
                reads=[("gtmp",)], writes=[("gp", slot, "e", ri)])
            b.op("DVE", lambda e, mk=mk, sc2=sc2: e.tensor_tensor(
                gtmp[:, :], sc2[:, :], mk[:, :], ALU.mult),
                reads=[("gsc2s", slot), ("gm1f", slot), ("gm2f", slot),
                       ("gp", slot, "e", ri)],
                writes=[("gtmp",)])
            b.op("DVE", lambda e, w8=w8, ri=ri: e.reduce_sum(
                w8[:, ri:ri + 1], gtmp[:, :], axis=AX.X),
                reads=[("gtmp",)], writes=[("gw", slot, ri)])
            b.op("DVE", lambda e, p8=p8, ri=ri: e.tensor_scalar(
                p8[:, 4 + ri:5 + ri], p8[:, ri:ri + 1], float(CAP), None, ALU.is_ge),
                reads=[("gp", slot, "p", ri)], writes=[("gp", slot, "ov", ri)])
            b.op("DVE", lambda e, p8=p8, ri=ri: e.tensor_scalar(
                p8[:, 6 + ri:7 + ri], p8[:, 2 + ri:3 + ri], float(CAP), None, ALU.mult),
                reads=[("gp", slot, "e", ri)], writes=[("gp", slot, "o", ri)])
            b.op("DVE", lambda e, p8=p8, ri=ri: e.tensor_tensor(
                p8[:, 6 + ri:7 + ri], p8[:, 6 + ri:7 + ri], p8[:, ri:ri + 1], ALU.add),
                reads=[("gp", slot, "o", ri), ("gp", slot, "p", ri)],
                writes=[("gp", slot, "o", ri)])
            b.op("DVE", lambda e, w8=w8, p8=p8, ri=ri: e.tensor_tensor(
                w8[:, 4 + ri:5 + ri], p8[:, 6 + ri:7 + ri], p8[:, 4 + ri:5 + ri], ALU.mult),
                reads=[("gp", slot, "o", ri), ("gp", slot, "ov", ri)],
                writes=[("gw", slot, "t", ri)])
            b.op("DVE", lambda e, w8=w8, p8=p8, ri=ri: e.tensor_tensor(
                p8[:, 6 + ri:7 + ri], p8[:, 6 + ri:7 + ri], w8[:, 4 + ri:5 + ri], ALU.subtract),
                reads=[("gp", slot, "o", ri), ("gw", slot, "t", ri)],
                writes=[("gp", slot, "o", ri)])
            b.op("DVE", lambda e, p8=p8, ri=ri: e.scalar_tensor_tensor(
                p8[:, 6 + ri:7 + ri], p8[:, 4 + ri:5 + ri], float(DUMP), p8[:, 6 + ri:7 + ri],
                ALU.mult, ALU.add),
                reads=[("gp", slot, "o", ri), ("gp", slot, "ov", ri)],
                writes=[("gp", slot, "o", ri)])
            b.op("DVE", lambda e, oi=oi, p8=p8, ri=ri: e.tensor_copy(
                oi[:, ri:ri + 1], p8[:, 6 + ri:7 + ri]),
                reads=[("gp", slot, "o", ri)], writes=[("goi", slot, ri)])
            b.op("DVE", lambda e, p8=p8, w8=w8, ri=ri: e.tensor_scalar(
                w8[:, 6 + ri:7 + ri], p8[:, 4 + ri:5 + ri], -1.0, 1.0,
                ALU.mult, ALU.add),
                reads=[("gp", slot, "ov", ri)], writes=[("gw", slot, "keep", ri)])
            b.op("DVE", lambda e, p8=p8, w8=w8, ri=ri: e.tensor_tensor(
                w8[:, 4 + ri:5 + ri], p8[:, 6 + ri:7 + ri], w8[:, 6 + ri:7 + ri],
                ALU.mult),
                reads=[("gp", slot, "o", ri), ("gw", slot, "keep", ri)],
                writes=[("gw", slot, "t2s", ri)])
            b.op("DVE", lambda e, w8=w8, ri=ri: e.tensor_tensor(
                w8[:, ri:ri + 1], w8[:, ri:ri + 1], w8[:, 6 + ri:7 + ri], ALU.mult),
                reads=[("gw", slot, ri), ("gw", slot, "keep", ri)],
                writes=[("gw", slot, ri)])
            b.op("DVE", lambda e, oi=oi, w8=w8, ri=ri: e.tensor_copy(
                oi[:, 3 + ri:4 + ri], w8[:, 4 + ri:5 + ri]),
                reads=[("gw", slot, "t2s", ri)], writes=[("goi", slot, "t2s", ri)])
        b.op("DVE", lambda e, w8=w8, tc=tc: e.tensor_scalar(
            w8[:, 2:3], iotacol_sb[:, :], float(tc * 128), None, ALU.add),
            reads=[("iotacol_sb",)], writes=[("gw", slot, "tok")])
        b.op("DVE", lambda e, oi=oi, w8=w8: e.tensor_copy(oi[:, 2:3], w8[:, 2:3]),
             reads=[("gw", slot, "tok")], writes=[("goi", slot, "tok")])
        dma("SP", tok2slot.ap()[tc * 128:(tc + 1) * 128, :], oi[:, 3:5],
            [("goi", slot, "t2s", 0), ("goi", slot, "t2s", 1)], [("tok2slot",)], "d_t2s")
        dma("SP", wtok.ap()[tc * 128:(tc + 1) * 128, :], w8[:, 0:2],
            [("gw", slot, 0), ("gw", slot, 1)], [("wtok",)], "d_wtok")
        b.op("POOL", lambda e, oi=oi, w8=w8: e.indirect_dma_start(
            bass.AP(idx_tab, 0, [[1, NE * CAP + 128], [1, 1]]),
            bass.IndirectOffsetOnAxis(oi[:, 0:1], 0),
            w8[:, 2:3], None),
            reads=[("goi", slot, 0), ("gw", slot, "tok")],
            writes=[("idx_tab",)], inc=("g_itab", 16, "swdge"))
        b.op("POOL", lambda e, oi=oi, w8=w8: e.indirect_dma_start(
            bass.AP(idx_tab, 0, [[1, NE * CAP + 128], [1, 1]]),
            bass.IndirectOffsetOnAxis(oi[:, 1:2], 0),
            w8[:, 2:3], None),
            reads=[("goi", slot, 1), ("gw", slot, "tok")],
            writes=[("idx_tab",)], inc=("g_itab", 16, "swdge"))
    arena_reset()
    if phases <= 9:
        b.emit(); ctx.close(); return nc

    wgu_sb = aalloc("wgu_sb", [128, 16 * GU], BF16)
    wdn_sb = aalloc("wdn_sb", [128, 11 * D], BF16)
    x1t_moe = [aalloc(f"x1tm{i}", [128, 16 * 512], BF16) for i in range(2)]
    h_sb = [aalloc(f"hsb{i}", [128, 11 * 512], BF16) for i in range(2)]
    gsil = aalloc("gsil", [128, 512], BF16)
    y_evm_s = aalloc("yevm", [128, D], F32)
    y_evm = [y_evm_s, y_evm_s]
    ones_col_f32 = aalloc("ones_col_f32", [128, 1], F32)
    b.op("POOL", lambda e: e.memset(ones_col_f32[:, :], 1.0), writes=[("ones_col_f32",)])

    NBLK = NE * CAP // 128
    idxt_sb = aalloc("idxt_sb", [128, NBLK], mybir.dt.int32)
    idxt_f = aalloc("idxt_f", [128, 128], F32)

    def stage_idxt():
        for half in range(NBLK // 128):
            dma("SP", idxt_f[:, :],
                bass.AP(idx_tab, half * 128 * 128, [[128, 128], [1, 128]]),
                [("idx_tab",), ("idxt_f",)], [("idxt_f",)], "d_its")
            b.op("PE", lambda e: e.transpose(
                PS[2][:, 0:128], idxt_f[:, :], ident_f32_sb[:, :]),
                reads=[("idxt_f",), ("ident_f32_sb",)],
                war_only=[("ps_idxt",), ("ps_dn", 0), ("ps_tot",)],
                writes=[("ps_idxt",)])
            b.op("DVE", lambda e, half=half: e.tensor_copy(
                idxt_sb[:, half * 128:(half + 1) * 128], PS[2][:, 0:128]),
                reads=[("ps_idxt",)], writes=[("idxt_sb",)])
    xg_tok = [aalloc(f"xg_tok{i}", [128, D], BF16) for i in range(2)]
    for i in range(2):
        b.op("POOL", lambda e, i=i: e.memset(xg_tok[i][:, :], 0.0),
             writes=[("xg_tok", i)])

    N_EXP = NE + 2
    first_ei = NE
    for ei in [NE, NE + 1] + list(range(NE)):
        if ei == 0:
            stage_idxt()
        for kc in range(16):
            dma("SP", wgu_sb[:, kc * GU:(kc + 1) * GU],
                wgu_bf.ap()[ei * 2048 + kc * 128:ei * 2048 + (kc + 1) * 128, :],
                [("wgu_bf", ei)], [("wgu", kc)], "d_wgu")
        for ic in range(11):
            dma("SP", wdn_sb[:, ic * D:(ic + 1) * D],
                wdn_bf.ap()[ei * 1408 + ic * 128:ei * 1408 + (ic + 1) * 128, :],
                [("wdn_bf", ei)], [("wdn", ic)], "d_wdn")
        if ei < NE:
            nj = CAP // 512
            for c in range(CAP // 128):
                gs = c % 2
                b.op("POOL", lambda e, c=c, ei=ei, gs=gs: e.indirect_dma_start(
                    xg_tok[gs][:, :], None,
                    x1bf.ap(), bass.IndirectOffsetOnAxis(
                        idxt_sb[:, ei * (CAP // 128) + c:ei * (CAP // 128) + c + 1], 0)),
                    reads=[("idxt_sb",)] + [("x1bf", i) for i in range(4)],
                    war_only=[("xg_tok", gs)], writes=[("xg_tok", gs)],
                    inc=(f"g_xg{gs}", 16, "swdge"))
                dma("POOL", xe_bf.ap()[(ei % 2) * CAP + c * 128:(ei % 2) * CAP + (c + 1) * 128, :],
                    xg_tok[gs][:, :], [("xg_tok", gs)], [("xe_bf", ei % 2)],
                    sem_name=f"g_xep{ei % 2}")
        else:
            nj = N_QT
        for qt in range(nj + 1):
            if qt < nj:
                hslot = qt % 2
                xtb = x1t_moe[qt % 2]
                if ei < NE:
                    for kc in range(16):
                        dma_t(xtb[:, kc * 512:(kc + 1) * 512],
                              xe_bf.ap()[(ei % 2) * CAP + qt * 512:(ei % 2) * CAP + (qt + 1) * 512,
                                         kc * 128:(kc + 1) * 128],
                              [("xe_bf", ei % 2)], [("x1t_moe", qt % 2)], "d_xgt",
                              queue="SP")
                else:
                    src = bass.AP(x1T, qt * 512, [[N_TOK, 128], [128 * N_TOK, 16], [1, 512]])
                    dma("SP", xtb[:, :], src, [("x1T", 0), ("x1T", 1)],
                        [("x1t_moe", qt % 2)], "d_mx")
                for m in range(22):
                    ps = PS[m % 2]
                    war0 = [("ps_gu", m % 2)]
                    if ei == first_ei and qt == 0 and m < 2:
                        war0 = war0 + [("ps_g",) if m == 0 else ("ps_cum",)]
                    for kc in range(16):
                        b.op("PE", lambda e, ps=ps, m=m, kc=kc, xtb=xtb: e.matmul(
                            ps[:, :], wgu_sb[:, kc * GU + m * 128:kc * GU + (m + 1) * 128],
                            xtb[:, kc * 512:(kc + 1) * 512],
                            start=(kc == 0), stop=(kc == 15)),
                            reads=[("wgu", kc), ("x1t_moe", qt % 2)],
                            war_only=war0 if kc == 0 else (),
                            writes=[("ps_gu", m % 2)] if kc == 15 else (),
                            no_inc=(kc != 15))
                    if m < 11:
                        b.op("ACT", lambda e, ps=ps: e.activation(gsil[:, :], ps[:, :], AF.Silu),
                             reads=[("ps_gu", m % 2)], writes=[("gsil",)])
                        b.op("DVE", lambda e, m=m, hslot=hslot: e.tensor_copy(
                            h_sb[hslot][:, m * 512:(m + 1) * 512], gsil[:, :]),
                            reads=[("gsil",)], writes=[("h", hslot)])
                    else:
                        im = m - 11
                        b.op("DVE", lambda e, ps=ps, im=im, hslot=hslot: e.tensor_tensor(
                            h_sb[hslot][:, im * 512:(im + 1) * 512],
                            h_sb[hslot][:, im * 512:(im + 1) * 512], ps[:, :], ALU.mult),
                            reads=[("h", hslot), ("ps_gu", m % 2)], writes=[("h", hslot)])
            if qt >= 1:
                dqt = qt - 1
                dslot = dqt % 2
                for t4 in range(4):
                    tcg = dqt * 4 + t4
                    for ic in range(11):
                        for jg in range(4):
                            ps = PS[2 + jg]
                            ward = [("ps_dn", jg)]
                            if ei == first_ei and dqt == 0 and t4 == 0 and jg == 0:
                                ward = ward + [("ps_tot",)]
                            b.op("PE", lambda e, ps=ps, ic=ic, t4=t4, jg=jg, dslot=dslot:
                                 e.matmul(
                                     ps[:, :],
                                     h_sb[dslot][:, ic * 512 + t4 * 128:
                                                 ic * 512 + (t4 + 1) * 128],
                                     wdn_sb[:, ic * D + jg * 512:ic * D + (jg + 1) * 512],
                                     start=(ic == 0), stop=(ic == 10)),
                                 reads=[("h", dslot), ("wdn", ic)],
                                 war_only=ward if ic == 0 else (),
                                 writes=[("ps_dn", jg)] if ic == 10 else (),
                                 no_inc=(ic != 10))
                    sc_ap = ones_col_f32[:, :]
                    sc_key = ("ones_col_f32",)
                    for jg in range(4):
                        ps = PS[2 + jg]
                        if jg % 2 == 0:
                            b.op("DVE", lambda e, ps=ps, jg=jg, t4=t4, sc_ap=sc_ap:
                                 e.tensor_scalar(y_evm[t4 % 2][:, jg * 512:(jg + 1) * 512],
                                                 ps[:, :], sc_ap, None, ALU.mult),
                                 reads=[("ps_dn", jg), sc_key],
                                 writes=[("y_evm",)])
                        else:
                            b.op("ACT", lambda e, ps=ps, jg=jg, t4=t4, sc_ap=sc_ap:
                                 e.activation(y_evm[t4 % 2][:, jg * 512:(jg + 1) * 512],
                                              ps[:, :], AF.Copy, scale=sc_ap),
                                 reads=[("ps_dn", jg), sc_key],
                                 writes=[("y_evm",)])
                    if ei < NE:
                        row0 = ei * CAP + dqt * 512 + t4 * 128
                        dma("ACT", y_slab.ap()[row0:row0 + 128, :],
                            y_evm[t4 % 2][:, :],
                            [("y_evm",)], [("y_slab", ei % 2)], "d_yslab")
                    else:
                        dma("POOL", yacc.ap()[tcg * 128:(tcg + 1) * 128, :],
                            y_evm[t4 % 2][:, :],
                            [("y_evm",)], [("yacc", tcg)], "g_yacc",
                            accum_op=ALU.add)
    arena_reset()
    if phases <= 10:
        b.emit(); ctx.close(); return nc

    yl = [aalloc(f"yl{i}", [128, D], F32) for i in range(2)]
    y1l = [aalloc(f"y1l{i}", [128, D], F32) for i in range(2)]
    y2l = [aalloc(f"y2l{i}", [128, D], F32) for i in range(2)]
    t2s_sb = [aalloc(f"t2s{i}", [128, 2], mybir.dt.int32) for i in range(2)]
    wv_sb = [aalloc(f"wv{i}", [128, 2], F32) for i in range(2)]
    sqs11 = aalloc("sqs11", [128, D], F32)
    stat11 = [aalloc(f"stat11_{i}", [128, 8], F32) for i in range(2)]
    slab_keys = [("y_slab", 0), ("y_slab", 1)]
    for tc in range(N_TC):
        slot = tc % 2
        dma("SP", yl[slot][:, :], yacc.ap()[tc * 128:(tc + 1) * 128, :],
            [("yl", slot), ("yacc", tc)], [("yl", slot)], "d_yl")
        dma("SP", t2s_sb[slot][:, :], tok2slot.ap()[tc * 128:(tc + 1) * 128, :],
            [("t2s", slot), ("tok2slot",)], [("t2s", slot)], "d_t2sl")
        dma("SP", wv_sb[slot][:, :], wtok.ap()[tc * 128:(tc + 1) * 128, :],
            [("wv", slot), ("wtok",)], [("wv", slot)], "d_wvl")
        b.op("POOL", lambda e, slot=slot: e.indirect_dma_start(
            y1l[slot][:, :], None,
            y_slab.ap(), bass.IndirectOffsetOnAxis(t2s_sb[slot][:, 0:1], 0)),
            reads=[("t2s", slot)] + slab_keys,
            war_only=[("y1l", slot)], writes=[("y1l", slot)],
            inc=(f"g_y1g{slot}", 16, "swdge"))
        b.op("POOL", lambda e, slot=slot: e.indirect_dma_start(
            y2l[slot][:, :], None,
            y_slab.ap(), bass.IndirectOffsetOnAxis(t2s_sb[slot][:, 1:2], 0)),
            reads=[("t2s", slot)] + slab_keys,
            war_only=[("y2l", slot)], writes=[("y2l", slot)],
            inc=(f"g_y2g{slot}", 16, "swdge"))
        rf = yl[slot]
        b.op("DVE", lambda e, rf=rf, slot=slot: e.scalar_tensor_tensor(
            rf[:, :], y1l[slot][:, :], wv_sb[slot][:, 0:1], rf[:, :],
            ALU.mult, ALU.add),
            reads=[("yl", slot), ("y1l", slot), ("wv", slot)], writes=[("yl", slot)])
        b.op("DVE", lambda e, rf=rf, slot=slot: e.scalar_tensor_tensor(
            rf[:, :], y2l[slot][:, :], wv_sb[slot][:, 1:2], rf[:, :],
            ALU.mult, ALU.add),
            reads=[("yl", slot), ("y2l", slot), ("wv", slot)], writes=[("yl", slot)])
        st11 = stat11[slot]
        ln_stats(st11, sqs11, rf[:, :], D, [("yl", slot)], ("s11", slot), ("sqs11",))
        b.op("DVE", lambda e, rf=rf, st11=st11: e.tensor_scalar(rf[:, :], rf[:, :], st11[:, 2:3],
                                                                st11[:, 7:8], ALU.subtract, ALU.mult),
             reads=[("yl", slot), (("s11", slot), "stat")], writes=[("yl", slot)])
        dma("SP", out.ap()[tc * 128:(tc + 1) * 128, :], rf[:, :],
            [("yl", slot)], [("out", tc)], "d_out")

    print("arena peak/cap:", ar["peak"], arena_bytes)
    b.barrier(engines=("SP",))
    b.emit()
    ctx.close()
    return nc


_nc_cache = {}


def kernel(**inputs):
    x = np.ascontiguousarray(np.asarray(inputs["L0_attn_x"], dtype=np.float32))
    B, T_, D_ = x.shape
    consts = build_host_constants()
    if "nc" not in _nc_cache:
        _nc_cache["nc"] = build_program()
    nc = _nc_cache["nc"]
    base = {k: np.ascontiguousarray(np.asarray(v, dtype=np.float32))
            for k, v in inputs.items() if k != "L0_attn_x"}
    in_maps = []
    for c in range(NCORES):
        m = {"L0_attn_x": np.ascontiguousarray(x[2 * c:2 * c + 2].reshape(N_TOK, D_))}
        m.update(base)
        m.update(consts)
        in_maps.append(m)
    res = run_bass_kernel_spmd(nc, in_maps, core_ids=list(range(NCORES)))
    outs = [np.asarray(res.results[c]["out"]).reshape(2, T_, D_) for c in range(NCORES)]
    return np.concatenate(outs, axis=0)
